# version 08
# baseline (speedup 1.0000x reference)
import contextlib
import math
import numpy as np
import concourse.bass as bass
import concourse.mybir as mybir
from concourse.bass_utils import run_bass_kernel_spmd

F32 = mybir.dt.float32
F32R = mybir.dt.float32r
BF16 = mybir.dt.bfloat16
FP8 = mybir.dt.float8e4
AF = mybir.ActivationFunctionType
ALU = mybir.AluOpType
DRM = mybir.MatmulPerfMode.DoubleRow

D = 2048
TPC = 1024
NCH = 16
NT = 8
NH = 16
NEG = -1.0e30
SM_SCALE = np.float32(1.0 / math.sqrt(128.0))
SM_SHIFT = np.float32(-16.0)
LN_EPS = 1e-5
LA_EPS = 1e-6
INV_D = 1.0 / 2048.0
F0 = 0.0
FN1 = -1.0
FNH = np.float32(-0.5)

_UPTO = 4

PAIRS = [[0, 1], [2, 3], [4, 5], [6, 7]]


class Bld:
    ENGS = ("te", "act", "dve", "gp", "sy")

    def __init__(self, nc):
        self.nc = nc
        self.q = {e: [] for e in self.ENGS}
        self.cnt = {}
        self.sem = {}

    def mksem(self, stack, name):
        self.sem[name] = stack.enter_context(self.nc.semaphore(name))
        self.cnt[name] = 0

    def w(self, eng, name, val):
        if val is None or val <= 0:
            return
        s = self.sem[name]
        self.q[eng].append(lambda E, s=s, v=val: E.wait_ge(s, v))

    def i(self, eng, fn, sem=None, n=1):
        if sem is None and eng in ("dve", "act"):
            sem = eng
        if sem is None:
            self.q[eng].append(fn)
            return None
        v = self.cnt[sem] = self.cnt[sem] + n
        s = self.sem[sem]
        self.q[eng].append(lambda E, fn=fn, s=s, n=n: fn(E).then_inc(s, n))
        return v

    def barrier(self):
        if not hasattr(self, "last_bar"):
            self.last_bar = {}
        vals = {e: self.cnt[e] for e in ("te", "act", "dve")}
        for s, v in self.cnt.items():
            if s.startswith("s_") and v > self.last_bar.get(s, 0):
                vals[s] = v
                self.last_bar[s] = v
        for e in self.ENGS:
            for o, v in vals.items():
                self.w(e, o, v)

    def replay(self, block):
        b = self

        @block.tensor
        def _(E):
            for fn in b.q["te"]:
                fn(E)

        @block.scalar
        def _(E):
            for fn in b.q["act"]:
                fn(E)

        @block.vector
        def _(E):
            for fn in b.q["dve"]:
                fn(E)

        @block.gpsimd
        def _(E):
            for fn in b.q["gp"]:
                fn(E)

        @block.sync
        def _(E):
            for fn in b.q["sy"]:
                fn(E)


def build_program(upto=_UPTO):
    nc = bass.Bass("TRN2", target_bir_lowering=False)

    d_x = nc.dram_tensor("xfm", [D, TPC], F32R, kind="ExternalInput")
    d_wqt = nc.dram_tensor("wqt", [16, 128, 2048], F32, kind="ExternalInput")
    d_wkvt = nc.dram_tensor("wkvt", [D, 1024], F32, kind="ExternalInput")
    d_bq = nc.dram_tensor("bq", [1, D], F32, kind="ExternalInput")
    d_bkv = nc.dram_tensor("bkv", [1, 1024], F32, kind="ExternalInput")
    d_wot = nc.dram_tensor("wot", [16, 128, 2048], F32, kind="ExternalInput")
    d_qpt = nc.dram_tensor("qpt", [16, 128, 2048], F32, kind="ExternalInput")
    d_kpt = nc.dram_tensor("kpt", [16, 128, 2048], F32, kind="ExternalInput")
    d_vpt = nc.dram_tensor("vpt", [D, D], F32, kind="ExternalInput")
    d_opt = nc.dram_tensor("opt", [16, 128, 2048], F32, kind="ExternalInput")
    d_wgdr = nc.dram_tensor("wgdr", [2, 64, 128, 2048], F32, kind="ExternalInput")
    d_wldr = nc.dram_tensor("wldr", [16, 128, 8192], F32, kind="ExternalInput")
    d_masks = nc.dram_tensor("masks", [8, 128, 512], BF16, kind="ExternalInput")
    d_ones = nc.dram_tensor("onesin", [128, 512], F32, kind="ExternalInput")
    d_y = nc.dram_tensor("yfm", [D, TPC], F32R, kind="ExternalOutput")

    d_ccw_in = nc.dram_tensor("ccw_in", [1, 128], mybir.dt.int32)
    d_ccw_out = nc.dram_tensor("ccw_out", [2, 128], mybir.dt.int32)
    d_cca_in = nc.dram_tensor("cca_in", [128, 516], F32)
    d_cca_out = nc.dram_tensor("cca_out", [256, 516], F32)
    d_ccb_in = nc.dram_tensor("ccb_in", [128, 32768], BF16)
    d_ccb_out = nc.dram_tensor("ccb_out", [256, 32768], BF16)
    d_stg = nc.dram_tensor("stash_g", [2, 64, 128, 2048], FP8)
    d_stl = nc.dram_tensor("stash_l", [16, 128, 8192], FP8)

    def dap(t, off, ap):
        return bass.AP(t, off, ap)

    with contextlib.ExitStack() as top:
        b = Bld(nc)
        E = top.enter_context
        semnames = ["te", "act", "dve", "gpi", "st", "cc",
                    "bar_te", "bar_act", "bar_dve", "bar_gp", "bar_sy",
                    "s_xin", "s_ones", "s_bias", "s_wkv0", "s_wkv1",
                    "s_wq0", "s_wq1", "s_ccaex", "s_ccarb",
                    "s_qp0", "s_qp1", "s_vp", "s_kx0", "s_kx1",
                    "s_kh0", "s_kh1", "s_msk"]
        for call in (1, 2):
            for sl in (0, 1):
                semnames += [f"s_wg{call}{sl}", f"s_wl{call}{sl}"]
        semnames += ["s_stg0", "s_stg1", "s_stl0", "s_stl1"]
        for s in semnames:
            b.mksem(top, s)

        x_sb = E(nc.sbuf_tensor("x_sb", [128, NCH * 1024], F32R))
        ones_c = E(nc.sbuf_tensor("ones_c", [128, 1], F32R))
        ones_r = E(nc.sbuf_tensor("ones_r", [1, 512], F32R))
        ones_cb = E(nc.sbuf_tensor("ones_cb", [128, 1], BF16))
        ones_rb = E(nc.sbuf_tensor("ones_rb", [1, 512], BF16))
        st_sb = E(nc.sbuf_tensor("st_sb", [1, 3072], F32R))
        scr_sb = E(nc.sbuf_tensor("scr_sb", [128, 1024], F32R))
        rb_sb = E(nc.sbuf_tensor("rb_sb", [128, 1024], F32))
        et2_sb = E(nc.sbuf_tensor("et2_sb", [128, 1024], BF16))

        rA = st_sb[0:1, 0:1024]
        rB = st_sb[0:1, 1024:2048]
        rC = st_sb[0:1, 2048:3072]
        nrow = st_sb[0:1, 0:1024]
        rrow = st_sb[0:1, 1024:2048]

        b.i("sy", lambda E: E.dma_start(
            out=x_sb.ap(),
            in_=dap(d_x, 0, [[1024, 128], [131072, 16], [1, 1024]])),
            sem="s_xin", n=16)
        b.i("gp", lambda E: E.dma_start(out=ones_c[:, :], in_=d_ones[:, 0:1], allow_slow_non_contiguous=True), sem="s_ones", n=16)
        b.i("gp", lambda E: E.dma_start(out=ones_r[:, :], in_=d_ones[0:1, :]), sem="s_ones", n=16)
        b.i("gp", lambda E: E.dma_start(out=ones_cb[:, :], in_=d_ones[:, 0:1], allow_slow_non_contiguous=True), sem="s_ones", n=16)
        b.i("gp", lambda E: E.dma_start(out=ones_rb[:, :], in_=d_ones[0:1, :]), sem="s_ones", n=16)
        for e in ("te", "act", "dve"):
            b.w(e, "s_xin", 16)
            b.w(e, "s_ones", 64)
        b.i("gp", lambda E: E.collective_compute(
            "AllGather", ALU.bypass, replica_groups=PAIRS,
            ins=[d_ccw_in.ap().opt()], outs=[d_ccw_out.ap().opt()]),
            sem="cc", n=1)

        def emit_ln(tag, dst_kind, xs=None, xs8=None, t1h=None):
            with contextlib.ExitStack() as ps:
                p_sx = ps.enter_context(nc.psum_tensor("psx" + tag, [1, 1024], F32))
                p_sxx = ps.enter_context(nc.psum_tensor("psxx" + tag, [1, 1024], F32))

                mm_done = []
                for k in range(32):
                    c, hf = divmod(k, 2)
                    sl = (k % 2) * 512
                    xh = x_sb[:, k * 512:(k + 1) * 512]
                    b.i("te", lambda E, hf=hf, xh=xh, c=c:
                        E.matmul(p_sx[0:1, hf * 512:(hf + 1) * 512], ones_c[:, :], xh,
                                 start=(c == 0), stop=(c == 15)),
                        sem=("te" if c == 15 else None))
                    if k >= 2:
                        b.w("act", "te", mm_done[k - 2])
                    vsq = b.i("act", lambda E, sl=sl, xh=xh:
                              E.activation(scr_sb[:, sl:sl + 512], xh, AF.Square),
                              sem="act")
                    b.w("te", "act", vsq)
                    mm_done.append(b.i("te", lambda E, hf=hf, sl=sl, c=c:
                                       E.matmul(p_sxx[0:1, hf * 512:(hf + 1) * 512],
                                                ones_c[:, :], scr_sb[:, sl:sl + 512],
                                                start=(c == 0), stop=(c == 15)),
                                       sem="te"))

                b.w("dve", "te", mm_done[-1])
                b.i("dve", lambda E: E.tensor_scalar_mul(rA, p_sx[0:1, :], INV_D))
                v2 = b.i("dve", lambda E: E.tensor_scalar_mul(rB, p_sxx[0:1, :], INV_D))
                b.w("dve", "dve", v2)
                v3 = b.i("dve", lambda E: E.scalar_tensor_tensor(
                    rC, rA, FN1, rA, ALU.mult, ALU.mult))
                b.w("dve", "dve", v3)
                vd = b.i("dve", lambda E: E.scalar_tensor_tensor(
                    rB, rB, LN_EPS, rC, ALU.add, ALU.add), sem="dve")
                b.w("act", "dve", vd)
                vL = b.i("act", lambda E: E.activation(rC, rB, AF.Ln))
                b.w("act", "act", vL)
                va = b.i("act", lambda E: E.activation(rB, rC, AF.Exp, scale=FNH),
                         sem="act")
                b.w("dve", "act", va)
                vn = b.i("dve", lambda E: E.tensor_scalar_mul(rC, rA, FN1),
                         sem="dve")

            with contextlib.ExitStack() as ps:
                p_bA = ps.enter_context(nc.psum_tensor("pbA" + tag, [128, 1024], F32))
                p_bB = ps.enter_context(nc.psum_tensor("pbB" + tag, [128, 1024], F32))
                b.w("te", "act", va)
                b.w("te", "dve", vn)
                vb = None
                for hf in range(2):
                    b.i("te", lambda E, hf=hf:
                        E.matmul(p_bA[:, hf * 512:(hf + 1) * 512], ones_r[0:1, 0:128],
                                 rB[0:1, hf * 512:(hf + 1) * 512], start=True, stop=True),
                        sem="te")
                    vb = b.i("te", lambda E, hf=hf:
                             E.matmul(p_bB[:, hf * 512:(hf + 1) * 512], ones_r[0:1, 0:128],
                                      rC[0:1, hf * 512:(hf + 1) * 512], start=True, stop=True),
                             sem="te")
                b.w("dve", "te", vb)
                for k in range(32):
                    c, hf = divmod(k, 2)
                    xh = x_sb[:, k * 512:(k + 1) * 512]
                    bAh = p_bA[:, hf * 512:(hf + 1) * 512]
                    bBh = p_bB[:, hf * 512:(hf + 1) * 512]
                    if dst_kind == "bf16":
                        dst = xs[:, k * 512:(k + 1) * 512]
                        v1 = b.i("dve", lambda E, dst=dst, xh=xh, bBh=bBh:
                                 E.tensor_add(dst, xh, bBh))
                        b.w("dve", "dve", v1)
                        b.i("dve", lambda E, dst=dst, bAh=bAh:
                            E.tensor_mul(dst, dst, bAh), sem="dve")
                    else:
                        sl = (k % 2) * 512
                        dst = xs8[:, c // 2, c % 2, hf * 512:(hf + 1) * 512]
                        v1 = b.i("dve", lambda E, sl=sl, xh=xh, bBh=bBh:
                                 E.tensor_add(t1h[:, sl:sl + 512], xh, bBh))
                        b.w("dve", "dve", v1)
                        b.i("dve", lambda E, dst=dst, sl=sl, bAh=bAh:
                            E.tensor_mul(dst, t1h[:, sl:sl + 512], bAh), sem="dve")
            return b.cnt["dve"]

        def emit_attn_a():
            with contextlib.ExitStack() as sb:
                xs = sb.enter_context(nc.sbuf_tensor("xsa", [128, NCH * 1024], BF16))
                sq = sb.enter_context(nc.sbuf_tensor("sqa", [128, NCH * 1024], BF16))
                wkv = sb.enter_context(nc.sbuf_tensor("wkva", [128, 2, 1024], BF16))
                wq = sb.enter_context(nc.sbuf_tensor("wqa", [128, 2, 2048], BF16))
                sk_sb = sb.enter_context(nc.sbuf_tensor("ska", [128, NT * 512], BF16))
                v_sb = sb.enter_context(nc.sbuf_tensor("va", [128, NT * 512], BF16))
                cca = sb.enter_context(nc.sbuf_tensor("cca", [128, 1032], F32))
                kvt = sb.enter_context(nc.sbuf_tensor("kvta", [128, 512], BF16))
                ksc = sb.enter_context(nc.sbuf_tensor("ksca", [128, 4], BF16))
                bq_sb = sb.enter_context(nc.sbuf_tensor("bqa", [1, 2048], BF16))
                bkv_sb = sb.enter_context(nc.sbuf_tensor("bkva", [1, 1024], BF16))

                emit_ln("a1", "bf16", xs=xs)
                b.barrier()

                b.i("gp", lambda E: E.dma_start(out=bq_sb[:, :], in_=d_bq.ap()),
                    sem="s_bias", n=16)
                b.i("gp", lambda E: E.dma_start(out=bkv_sb[:, :], in_=d_bkv.ap()),
                    sem="s_bias", n=16)
                b.w("te", "s_bias", 32)

                with contextlib.ExitStack() as ps:
                    p_kv = [ps.enter_context(nc.psum_tensor(f"pkv{i}", [128, 1024], F32))
                            for i in range(4)]
                    ld = []
                    for rep in range(2):
                        for c in range(NCH):
                            k = rep * NCH + c
                            sl = k % 2
                            if k >= 2:
                                b.w("gp", "te", ld[k - 2][1])
                            vld = b.i("gp", lambda E, c=c, sl=sl: E.dma_start(
                                out=wkv[:, sl, :],
                                in_=dap(d_wkvt, c * 131072, [[1024, 128], [1, 1024]])),
                                sem=f"s_wkv{sl}", n=16)
                            b.w("te", f"s_wkv{sl}", vld)
                            vte = None
                            for tt4 in range(4):
                                tt = rep * 4 + tt4
                                lhs = xs[:, c * 1024 + tt * 128: c * 1024 + tt * 128 + 128]
                                for hf in range(2):
                                    vte = b.i("te", lambda E, tt4=tt4, sl=sl, hf=hf, c=c, lhs=lhs:
                                              E.matmul(p_kv[tt4][:, hf * 512:(hf + 1) * 512],
                                                       lhs,
                                                       wkv[:, sl, hf * 512:(hf + 1) * 512],
                                                       start=(c == 0), stop=False),
                                              sem=("te" if (tt4 == 3 and hf == 1) else None))
                            ld.append((vld, vte))
                        for tt4 in range(4):
                            tt = rep * 4 + tt4
                            vte = None
                            for hf in range(2):
                                vte = b.i("te", lambda E, tt4=tt4, hf=hf:
                                          E.matmul(p_kv[tt4][:, hf * 512:(hf + 1) * 512],
                                                   ones_rb[0:1, 0:128],
                                                   bkv_sb[0:1, hf * 512:(hf + 1) * 512],
                                                   start=False, stop=True),
                                          sem="te")
                            esl = (tt % 2) * 512
                            b.w("dve", "te", vte)
                            vmin = b.i("dve", lambda E, tt4=tt4, esl=esl:
                                       E.tensor_scalar_min(scr_sb[:, esl:esl + 512],
                                                           p_kv[tt4][:, 0:512], F0),
                                       sem="dve")
                            b.w("act", "dve", vmin)
                            vexp = b.i("act", lambda E, esl=esl:
                                       E.activation(et2_sb[:, esl:esl + 512],
                                                    scr_sb[:, esl:esl + 512], AF.Exp),
                                       sem="act")
                            b.w("act", "te", vte)
                            vvc = b.i("act", lambda E, tt4=tt4, tt=tt:
                                      E.copy(v_sb[:, tt * 512:(tt + 1) * 512],
                                             p_kv[tt4][:, 512:1024]), sem="act")
                            b.w("dve", "act", vexp)
                            vsk = b.i("dve", lambda E, tt4=tt4, tt=tt, esl=esl:
                                      E.scalar_tensor_tensor(
                                          sk_sb[:, tt * 512:(tt + 1) * 512],
                                          p_kv[tt4][:, 0:512], F0,
                                          et2_sb[:, esl:esl + 512],
                                          ALU.max, ALU.add), sem="dve")
                        if rep == 0:
                            b.w("te", "dve", b.cnt["dve"])
                            b.w("te", "act", b.cnt["act"])
                    vkv_done = (b.cnt["dve"], b.cnt["act"])

                with contextlib.ExitStack() as ps:
                    p_kk = ps.enter_context(nc.psum_tensor("pkk", [128, 1024], F32))
                    b.w("te", "dve", vkv_done[0])
                    b.w("te", "act", vkv_done[1])
                    vkk = None
                    for g in range(4):
                        for tt in range(NT):
                            lhs = sk_sb[:, tt * 512 + g * 128: tt * 512 + (g + 1) * 128]
                            b.i("te", lambda E, g=g, tt=tt, lhs=lhs:
                                E.matmul(p_kk[:, g * 128:(g + 1) * 128], lhs,
                                         v_sb[:, tt * 512 + g * 128: tt * 512 + (g + 1) * 128],
                                         start=(tt == 0), stop=(tt == 7)),
                                sem=("te" if tt == 7 else None))
                        for tt in range(NT):
                            lhs = sk_sb[:, tt * 512 + g * 128: tt * 512 + (g + 1) * 128]
                            vkk = b.i("te", lambda E, g=g, tt=tt, lhs=lhs:
                                      E.matmul(p_kk[:, 512 + g: 513 + g], lhs,
                                               ones_cb[:, :],
                                               start=(tt == 0), stop=(tt == 7)),
                                      sem="te")
                    b.w("dve", "te", vkk)
                    vkevac = b.i("dve", lambda E:
                                 E.tensor_copy(cca[:, 0:516], p_kk[:, 0:516]),
                                 sem="dve")

                b.w("gp", "dve", vkevac)
                vex = b.i("gp", lambda E: E.dma_start(out=d_cca_in.ap(),
                                                      in_=cca[:, 0:516]),
                          sem="s_ccaex", n=16)
                b.w("gp", "s_ccaex", vex)
                vcc = b.i("gp", lambda E: E.collective_compute(
                    "AllGather", ALU.bypass, replica_groups=PAIRS,
                    ins=[d_cca_in.ap().opt()], outs=[d_cca_out.ap().opt()]),
                    sem="cc", n=1)

                with contextlib.ExitStack() as ps:
                    p_q = [ps.enter_context(nc.psum_tensor(f"pq{i}", [128, 512], F32))
                           for i in range(2)]
                    b.w("te", "dve", vkevac)
                    q_te = []
                    q_ev = []
                    for h in range(NH):
                        sl = h % 2
                        if h >= 2:
                            b.w("gp", "te", q_te[2 * (h - 2) + 1])
                        vld = b.i("gp", lambda E, h=h, sl=sl: E.dma_start(
                            out=wq[:, sl, :],
                            in_=dap(d_wqt, h * 128 * 2048, [[2048, 128], [1, 2048]])),
                            sem=f"s_wq{sl}", n=16)
                        b.w("te", f"s_wq{sl}", vld)
                        for T in range(2):
                            gsl = (2 * h + T) % 2
                            if len(q_ev) >= 2:
                                b.w("te", "dve", q_ev[-2])
                            for c in range(NCH):
                                b.i("te", lambda E, gsl=gsl, sl=sl, c=c, T=T:
                                    E.matmul(p_q[gsl][:, :],
                                             wq[:, sl, c * 128:(c + 1) * 128],
                                             xs[:, c * 1024 + T * 512: c * 1024 + (T + 1) * 512],
                                             start=(c == 0), stop=False))
                            vq = b.i("te", lambda E, gsl=gsl, h=h:
                                     E.matmul(p_q[gsl][:, :], bq_sb[0:1, h * 128:(h + 1) * 128],
                                              ones_rb[0:1, 0:512], start=False, stop=True),
                                     sem="te")
                            q_te.append(vq)
                            esl = gsl * 512
                            b.w("dve", "te", vq)
                            vmin = b.i("dve", lambda E, gsl=gsl, esl=esl:
                                       E.tensor_scalar_min(scr_sb[:, esl:esl + 512],
                                                           p_q[gsl][:, :], F0), sem="dve")
                            b.w("act", "dve", vmin)
                            vexp = b.i("act", lambda E, esl=esl:
                                       E.activation(et2_sb[:, esl:esl + 512],
                                                    scr_sb[:, esl:esl + 512], AF.Exp),
                                       sem="act")
                            b.w("dve", "act", vexp)
                            q_ev.append(b.i("dve", lambda E, gsl=gsl, esl=esl, h=h, T=T:
                                            E.scalar_tensor_tensor(
                                                sq[:, h * 1024 + T * 512: h * 1024 + (T + 1) * 512],
                                                p_q[gsl][:, :], F0, et2_sb[:, esl:esl + 512],
                                                ALU.max, ALU.add), sem="dve"))
                    vq_all = q_ev[-1]

                b.w("gp", "cc", vcc)
                b.i("gp", lambda E: E.dma_start(out=cca[:, 0:516],
                                                in_=d_cca_out[0:128, :]),
                    sem="s_ccarb", n=16)
                vrb = b.i("gp", lambda E: E.dma_start(out=cca[:, 516:1032],
                                                      in_=d_cca_out[128:256, :]),
                          sem="s_ccarb", n=16)
                b.w("dve", "s_ccarb", vrb)
                vadd = b.i("dve", lambda E: E.tensor_add(cca[:, 0:516], cca[:, 0:516],
                                                         cca[:, 516:1032]))
                b.w("dve", "dve", vadd)
                b.i("dve", lambda E: E.tensor_copy(kvt[:, :], cca[:, 0:512]))
                vtot = b.i("dve", lambda E: E.tensor_copy(ksc[:, :], cca[:, 512:516]),
                           sem="dve")

                with contextlib.ExitStack() as ps:
                    p_qk = [ps.enter_context(nc.psum_tensor(f"pqk{i}", [128, 1024], F32))
                            for i in range(2)]
                    p_bc = ps.enter_context(nc.psum_tensor("pbca", [128, 1024], F32))
                    p_n = ps.enter_context(nc.psum_tensor("pn", [1, 1024], F32))
                    b.w("te", "dve", vtot)
                    b.w("te", "dve", vq_all)
                    vnr_last = 0
                    vcp_last = 0
                    sc_done = []
                    for h in range(NH):
                        g = h // 4
                        qsl = h % 2
                        b.w("te", "dve", vnr_last)
                        vn = None
                        for T in range(2):
                            vn = b.i("te", lambda E, T=T, g=g, h=h:
                                     E.matmul(p_n[0:1, T * 512:(T + 1) * 512],
                                              ksc[:, g:g + 1],
                                              sq[:, h * 1024 + T * 512: h * 1024 + (T + 1) * 512],
                                              start=True, stop=True), sem="te")
                        if len(sc_done) >= 2:
                            b.w("te", "dve", sc_done[-2])
                        vqk = None
                        for T in range(2):
                            vqk = b.i("te", lambda E, T=T, g=g, h=h, qsl=qsl:
                                      E.matmul(p_qk[qsl][:, T * 512:(T + 1) * 512],
                                               kvt[:, g * 128:(g + 1) * 128],
                                               sq[:, h * 1024 + T * 512: h * 1024 + (T + 1) * 512],
                                               start=True, stop=True), sem="te")
                        b.w("dve", "te", vn)
                        vna = b.i("dve", lambda E:
                                  E.tensor_scalar_add(nrow[0:1, 0:1024],
                                                      p_n[0:1, :], LA_EPS))
                        b.w("dve", "dve", vna)

                        def _recip(E):
                            with nc.allow_low_precision(reason="f32r recip row"):
                                return E.reciprocal(rrow[0:1, 0:1024], nrow[0:1, 0:1024])
                        vnr_last = b.i("dve", _recip, sem="dve")
                        b.w("te", "dve", vnr_last)
                        b.w("te", "dve", vcp_last)
                        vbc = None
                        for T in range(2):
                            vbc = b.i("te", lambda E, T=T:
                                      E.matmul(p_bc[:, T * 512:(T + 1) * 512],
                                               ones_r[0:1, 0:128],
                                               rrow[0:1, T * 512:(T + 1) * 512],
                                               start=True, stop=True), sem="te")
                        b.w("dve", "te", vbc)
                        b.w("dve", "te", vqk)
                        vcp_last = b.i("dve", lambda E:
                                       E.tensor_copy(rb_sb[:, 0:1024], p_bc[:, :]),
                                       sem="dve")
                        b.w("dve", "dve", vcp_last)
                        sc_done.append(b.i("dve", lambda E, h=h, qsl=qsl:
                                           E.scalar_tensor_tensor(
                                               sq[:, h * 1024: (h + 1) * 1024],
                                               p_qk[qsl][:, :], F0,
                                               rb_sb[:, 0:1024],
                                               ALU.bypass, ALU.mult), sem="dve"))
                    vattn = sc_done[-1]

                with contextlib.ExitStack() as ps:
                    p_o = [ps.enter_context(nc.psum_tensor(f"poa{i}", [128, 512], F32))
                           for i in range(2)]
                    o_te = []
                    o_ev = []
                    for oc in range(NCH):
                        sl = oc % 2
                        if oc >= 2:
                            b.w("gp", "te", o_te[2 * (oc - 2) + 1])
                        else:
                            b.w("gp", "te", q_te[2 * (14 + oc) + 1])
                        vld = b.i("gp", lambda E, oc=oc, sl=sl: E.dma_start(
                            out=wq[:, sl, :],
                            in_=dap(d_wot, oc * 128 * 2048, [[2048, 128], [1, 2048]])),
                            sem=f"s_wq{sl}", n=16)
                        b.w("te", f"s_wq{sl}", vld)
                        if oc == 0:
                            b.w("te", "dve", vattn)
                        for T in range(2):
                            gsl = (2 * oc + T) % 2
                            if len(o_ev) >= 2:
                                b.w("te", "dve", o_ev[-2])
                            vwo = None
                            for c in range(NCH):
                                vwo = b.i("te", lambda E, gsl=gsl, sl=sl, c=c, T=T:
                                          E.matmul(p_o[gsl][:, :],
                                                   wq[:, sl, c * 128:(c + 1) * 128],
                                                   sq[:, c * 1024 + T * 512: c * 1024 + (T + 1) * 512],
                                                   start=(c == 0), stop=(c == 15)),
                                          sem=("te" if c == 15 else None))
                            o_te.append(vwo)
                            b.w("dve", "te", vwo)
                            o_ev.append(b.i("dve", lambda E, gsl=gsl, oc=oc, T=T:
                                            E.tensor_add(
                                                x_sb[:, oc * 1024 + T * 512: oc * 1024 + (T + 1) * 512],
                                                x_sb[:, oc * 1024 + T * 512: oc * 1024 + (T + 1) * 512],
                                                p_o[gsl][:, :]), sem="dve"))

        def emit_ffn(call):
            tag = f"f{call}"
            with contextlib.ExitStack() as sb:
                xs8 = sb.enter_context(nc.sbuf_tensor("xs8" + tag, [128, 8, 2, 1024], FP8))
                t1h = sb.enter_context(nc.sbuf_tensor("t1h" + tag, [128, 1024], F32))
                act8 = sb.enter_context(nc.sbuf_tensor("act8" + tag, [128, 32, 2, 1024], FP8))
                wg0 = sb.enter_context(nc.sbuf_tensor("wg0" + tag, [128, 2, 8, 2, 128], FP8))
                wg1 = sb.enter_context(nc.sbuf_tensor("wg1" + tag, [128, 2, 8, 2, 128], FP8))
                wl = sb.enter_context(nc.sbuf_tensor("wl" + tag, [128, 2, 32, 2, 128], FP8))
                gtmp = sb.enter_context(nc.sbuf_tensor("gt" + tag, [128, 1024], BF16))

                emit_ln(tag, "fp8", xs8=xs8, t1h=t1h)
                b.barrier()

                with contextlib.ExitStack() as ps:
                    p_g = [ps.enter_context(nc.psum_tensor(f"pg{i}" + tag, [128, 512], F32))
                           for i in range(2)]
                    p_f = [ps.enter_context(nc.psum_tensor(f"pf{i}" + tag, [128, 512], F32))
                           for i in range(2)]

                    g_te = []
                    g_st = []
                    g_act = []
                    g_dve = []
                    for hc in range(64):
                        sl = hc % 2
                        wsem = f"s_wg{call}{sl}"
                        if hc >= 2:
                            b.w("gp" if call == 1 else "sy", "te", g_te[hc - 2])
                            if call == 1:
                                b.w("gp", f"s_stg{sl}", g_st[hc - 2])
                        w0ap = dap(wg0, sl * 2048, [[4096, 128], [1, 2048]])
                        w1ap = dap(wg1, sl * 2048, [[4096, 128], [1, 2048]])
                        if call == 1:
                            b.i("gp", lambda E, hc=hc, w0ap=w0ap: E.dma_start(
                                out=w0ap,
                                in_=dap(d_wgdr, hc * 128 * 2048, [[2048, 128], [1, 2048]])),
                                sem=wsem, n=16)
                            vld = b.i("gp", lambda E, hc=hc, w1ap=w1ap: E.dma_start(
                                out=w1ap,
                                in_=dap(d_wgdr, (64 + hc) * 128 * 2048,
                                        [[2048, 128], [1, 2048]])),
                                sem=wsem, n=16)
                            b.w("sy", wsem, vld)
                            b.i("sy", lambda E, hc=hc, w0ap=w0ap: E.dma_start(
                                out=dap(d_stg, hc * 128 * 2048, [[2048, 128], [1, 2048]]),
                                in_=w0ap), sem=f"s_stg{sl}", n=16)
                            g_st.append(b.i("sy", lambda E, hc=hc, w1ap=w1ap: E.dma_start(
                                out=dap(d_stg, (64 + hc) * 128 * 2048,
                                        [[2048, 128], [1, 2048]]),
                                in_=w1ap), sem=f"s_stg{sl}", n=16))
                            b.w("te", wsem, vld)
                        else:
                            b.i("sy", lambda E, hc=hc, w0ap=w0ap: E.dma_start(
                                out=w0ap,
                                in_=dap(d_stg, hc * 128 * 2048, [[2048, 128], [1, 2048]])),
                                sem=wsem, n=16)
                            vld = b.i("sy", lambda E, hc=hc, w1ap=w1ap: E.dma_start(
                                out=w1ap,
                                in_=dap(d_stg, (64 + hc) * 128 * 2048,
                                        [[2048, 128], [1, 2048]])),
                                sem=wsem, n=16)
                            b.w("te", wsem, vld)

                        vte = None
                        for T in range(2):
                            gsl = (2 * hc + T) % 2
                            if len(g_dve) >= 2:
                                b.w("te", "dve", g_dve[-2])
                                b.w("te", "act", g_act[-2])
                            for i in range(8):
                                b.i("te", lambda E, gsl=gsl, sl=sl, i=i, T=T:
                                    E.matmul(p_g[gsl][:, :], wg0[:, sl, i, :, :],
                                             xs8[:, i, :, T * 512:(T + 1) * 512],
                                             start=(i == 0), stop=(i == 7),
                                             perf_mode=DRM),
                                    sem=("te" if i == 7 else None))
                            for i in range(8):
                                vte = b.i("te", lambda E, gsl=gsl, sl=sl, i=i, T=T:
                                          E.matmul(p_f[gsl][:, :], wg1[:, sl, i, :, :],
                                                   xs8[:, i, :, T * 512:(T + 1) * 512],
                                                   start=(i == 0), stop=(i == 7),
                                                   perf_mode=DRM),
                                          sem=("te" if i == 7 else None))
                            b.w("act", "te", vte)
                            g_act.append(b.i("act", lambda E, gsl=gsl:
                                             E.activation(gtmp[:, gsl * 512:(gsl + 1) * 512],
                                                          p_g[gsl][:, :], AF.Gelu_apprx_tanh),
                                             sem="act"))
                            b.w("dve", "act", g_act[-1])
                            g_dve.append(b.i("dve", lambda E, gsl=gsl, hc=hc, T=T:
                                             E.tensor_mul(
                                                 act8[:, hc // 2, hc % 2, T * 512:(T + 1) * 512],
                                                 gtmp[:, gsl * 512:(gsl + 1) * 512],
                                                 p_f[gsl][:, :]), sem="dve"))
                        g_te.append(vte)
                    v_act8 = g_dve[-1]

                with contextlib.ExitStack() as ps:
                    p_o = [ps.enter_context(nc.psum_tensor(f"pfo{i}" + tag, [128, 512], F32))
                           for i in range(2)]
                    o_te = []
                    o_st = []
                    o_dve = []
                    for oc in range(NCH):
                        sl = oc % 2
                        lsem = f"s_wl{call}{sl}"
                        if oc >= 2:
                            b.w("gp" if call == 1 else "sy", "te", o_te[2 * (oc - 2) + 1])
                            if call == 1:
                                b.w("gp", f"s_stl{sl}", o_st[oc - 2])
                        wlap = dap(wl, sl * 8192, [[16384, 128], [1, 8192]])
                        if call == 1:
                            vld = b.i("gp", lambda E, oc=oc, wlap=wlap: E.dma_start(
                                out=wlap,
                                in_=dap(d_wldr, oc * 128 * 8192, [[8192, 128], [1, 8192]])),
                                sem=lsem, n=16)
                            b.w("sy", lsem, vld)
                            o_st.append(b.i("sy", lambda E, oc=oc, wlap=wlap: E.dma_start(
                                out=dap(d_stl, oc * 128 * 8192, [[8192, 128], [1, 8192]]),
                                in_=wlap), sem=f"s_stl{sl}", n=16))
                            b.w("te", lsem, vld)
                        else:
                            vld = b.i("sy", lambda E, oc=oc, wlap=wlap: E.dma_start(
                                out=wlap,
                                in_=dap(d_stl, oc * 128 * 8192, [[8192, 128], [1, 8192]])),
                                sem=lsem, n=16)
                            b.w("te", lsem, vld)
                        if oc == 0:
                            b.w("te", "dve", v_act8)
                        for T in range(2):
                            gsl = (2 * oc + T) % 2
                            if len(o_dve) >= 2:
                                b.w("te", "dve", o_dve[-2])
                            vte = None
                            for i in range(32):
                                vte = b.i("te", lambda E, gsl=gsl, sl=sl, i=i, T=T:
                                          E.matmul(p_o[gsl][:, :], wl[:, sl, i, :, :],
                                                   act8[:, i, :, T * 512:(T + 1) * 512],
                                                   start=(i == 0), stop=(i == 31),
                                                   perf_mode=DRM),
                                          sem=("te" if i == 31 else None))
                            o_te.append(vte)
                            b.w("dve", "te", vte)
                            o_dve.append(b.i("dve", lambda E, gsl=gsl, oc=oc, T=T:
                                             E.tensor_add(
                                                 x_sb[:, oc * 1024 + T * 512: oc * 1024 + (T + 1) * 512],
                                                 x_sb[:, oc * 1024 + T * 512: oc * 1024 + (T + 1) * 512],
                                                 p_o[gsl][:, :]), sem="dve"))
                            if call == 2:
                                b.w("gp", "dve", o_dve[-1])
                                b.i("gp", lambda E, oc=oc, T=T: E.dma_start(
                                    out=dap(d_y, oc * 128 * 1024 + T * 512,
                                            [[1024, 128], [1, 512]]),
                                    in_=x_sb[:, oc * 1024 + T * 512: oc * 1024 + (T + 1) * 512]),
                                    sem="st", n=16)

        def emit_attn_b():
            with contextlib.ExitStack() as sb:
                xs = sb.enter_context(nc.sbuf_tensor("xsb", [128, NCH * 1024], BF16))
                sq = sb.enter_context(nc.sbuf_tensor("sqb", [128, NCH * 1024], BF16))
                qp = sb.enter_context(nc.sbuf_tensor("qpb", [128, 2, 2048], BF16))
                kh = sb.enter_context(nc.sbuf_tensor("khb", [128, 2, 2048], BF16))
                vh = sb.enter_context(nc.sbuf_tensor("vhb", [128, 2, 2048], BF16))
                sb2 = sb.enter_context(contextlib.ExitStack())
                vps = sb2.enter_context(nc.sbuf_tensor("vpb", [128, NCH * 512], BF16))
                ktmp = sb2.enter_context(nc.sbuf_tensor("ktb", [128, 2, 512], BF16))

                emit_ln("a3", "bf16", xs=xs)
                b.barrier()

                with contextlib.ExitStack() as ps:
                    p_p = [ps.enter_context(nc.psum_tensor(f"pbp{i}", [128, 512], F32))
                           for i in range(2)]
                    kv_ev = []
                    kx = []

                    kb_te = []
                    for oc in range(NCH):
                        sl = oc % 2
                        if oc >= 2:
                            b.w("gp", "te", kb_te[2 * (oc - 2) + 1])
                        vld = b.i("gp", lambda E, oc=oc, sl=sl: E.dma_start(
                            out=qp[:, sl, :],
                            in_=dap(d_kpt, oc * 128 * 2048, [[2048, 128], [1, 2048]])),
                            sem=f"s_qp{sl}", n=16)
                        b.w("te", f"s_qp{sl}", vld)
                        for T in range(2):
                            gsl = (2 * oc + T) % 2
                            if len(kv_ev) >= 2:
                                b.w("te", "act", kv_ev[-2])
                            vte = None
                            for c in range(NCH):
                                vte = b.i("te", lambda E, gsl=gsl, sl=sl, c=c, T=T:
                                          E.matmul(p_p[gsl][:, :],
                                                   qp[:, sl, c * 128:(c + 1) * 128],
                                                   xs[:, c * 1024 + T * 512: c * 1024 + (T + 1) * 512],
                                                   start=(c == 0), stop=(c == 15)),
                                          sem=("te" if c == 15 else None))
                            kb_te.append(vte)
                            b.w("act", "te", vte)
                            if len(kx) >= 2:
                                b.w("act", f"s_kx{gsl}", kx[-2])
                            kv_ev.append(b.i("act", lambda E, gsl=gsl:
                                             E.copy(ktmp[:, gsl, :], p_p[gsl][:, :]), sem="act"))
                            b.w("sy", "act", kv_ev[-1])
                            kx.append(b.i("sy", lambda E, gsl=gsl, oc=oc, T=T: E.dma_start(
                                out=d_ccb_in[:, oc * 1024 + T * 512: oc * 1024 + (T + 1) * 512],
                                in_=ktmp[:, gsl, :]), sem=f"s_kx{gsl}", n=16))

                    for og in range(4):
                        if og > 0:
                            b.w("gp", "te", b.cnt["te"])
                        vld = b.i("gp", lambda E, og=og: E.dma_start(
                            out=vps.ap(),
                            in_=dap(d_vpt, og * 512,
                                    [[2048, 128], [262144, 16], [1, 512]])),
                            sem="s_vp", n=16)
                        b.w("te", "s_vp", vld)
                        for tt in range(NT):
                            gsl = (4 * og + tt) % 2
                            if len(kv_ev) >= 2:
                                b.w("te", "act", kv_ev[-2])
                            vte = None
                            for c in range(NCH):
                                lhs = xs[:, c * 1024 + tt * 128: c * 1024 + tt * 128 + 128]
                                vte = b.i("te", lambda E, gsl=gsl, c=c, lhs=lhs:
                                          E.matmul(p_p[gsl][:, :], lhs,
                                                   vps[:, c * 512:(c + 1) * 512],
                                                   start=(c == 0), stop=(c == 15)),
                                          sem=("te" if c == 15 else None))
                            b.w("act", "te", vte)
                            if len(kx) >= 2:
                                b.w("act", f"s_kx{gsl}", kx[-2])
                            kv_ev.append(b.i("act", lambda E, gsl=gsl:
                                             E.copy(ktmp[:, gsl, :], p_p[gsl][:, :]), sem="act"))
                            b.w("sy", "act", kv_ev[-1])
                            kx.append(b.i("sy", lambda E, gsl=gsl, og=og, tt=tt: E.dma_start(
                                out=dap(d_ccb_in, 16384 + 1024 * (4 * og) + 128 * tt,
                                        [[32768, 128], [1024, 4], [1, 128]]),
                                in_=ktmp[:, gsl, :]), sem=f"s_kx{gsl}", n=16))

                    b.w("gp", "s_kx0", b.cnt["s_kx0"])
                    b.w("gp", "s_kx1", b.cnt["s_kx1"])
                    vccb = b.i("gp", lambda E: E.collective_compute(
                        "AllGather", ALU.bypass, replica_groups=PAIRS,
                        ins=[d_ccb_in.ap().opt()], outs=[d_ccb_out.ap().opt()]),
                        sem="cc", n=1)

                    qb_te = []
                    q_ev = []
                    for oc in range(NCH):
                        sl = oc % 2
                        if oc >= 2:
                            b.w("gp", "te", qb_te[2 * (oc - 2) + 1])
                        else:
                            b.w("gp", "te", kb_te[2 * (14 + oc) + 1])
                        vld = b.i("gp", lambda E, oc=oc, sl=sl: E.dma_start(
                            out=qp[:, sl, :],
                            in_=dap(d_qpt, oc * 128 * 2048, [[2048, 128], [1, 2048]])),
                            sem=f"s_qp{sl}", n=16)
                        b.w("te", f"s_qp{sl}", vld)
                        for T in range(2):
                            gsl = (2 * oc + T) % 2
                            if len(q_ev) >= 2:
                                b.w("te", "act", q_ev[-2])
                            vte = None
                            for c in range(NCH):
                                vte = b.i("te", lambda E, gsl=gsl, sl=sl, c=c, T=T:
                                          E.matmul(p_p[gsl][:, :],
                                                   qp[:, sl, c * 128:(c + 1) * 128],
                                                   xs[:, c * 1024 + T * 512: c * 1024 + (T + 1) * 512],
                                                   start=(c == 0), stop=(c == 15)),
                                          sem=("te" if c == 15 else None))
                            qb_te.append(vte)
                            b.w("act", "te", vte)
                            q_ev.append(b.i("act", lambda E, gsl=gsl, oc=oc, T=T:
                                            E.copy(sq[:, oc * 1024 + T * 512: oc * 1024 + (T + 1) * 512],
                                                   p_p[gsl][:, :]), sem="act"))
                    vq_all = q_ev[-1]

                sb2.close()
                pt = sb.enter_context(nc.sbuf_tensor("ptb", [128, 16, 512], BF16))
                msk = sb.enter_context(nc.sbuf_tensor("mskb", [128, 8 * 512], BF16))
                b.w("sy", "s_kx0", b.cnt["s_kx0"])
                b.w("sy", "s_kx1", b.cnt["s_kx1"])
                vmask = b.i("sy", lambda E: E.dma_start(
                    out=msk.ap(),
                    in_=dap(d_masks, 0, [[512, 128], [65536, 8], [1, 512]])),
                    sem="s_msk", n=16)

                with contextlib.ExitStack() as ps:
                    p_st = [ps.enter_context(nc.psum_tensor(f"pst{i}", [128, 512], F32))
                            for i in range(3)]
                    p_ov = [ps.enter_context(nc.psum_tensor(f"pov{i}", [128, 512], F32))
                            for i in range(2)]
                    p_den = ps.enter_context(nc.psum_tensor("pden", [1, 512], F32))
                    p_bc = ps.enter_context(nc.psum_tensor("pbcb", [128, 512], F32))

                    b.w("te", "act", vq_all)
                    b.w("dve", "s_msk", vmask)
                    hb_te = []
                    hload = []
                    vden_last = 0
                    vrr_last = 0
                    vrc_last = 0
                    mden = []
                    for h in range(NH):
                        sl = h % 2
                        if h == 0:
                            b.w("sy", "cc", vccb)
                        if h >= 2:
                            b.w("sy", "te", hb_te[h - 2])
                        for q2 in range(2):
                            b.i("sy", lambda E, h=h, sl=sl, q2=q2: E.dma_start(
                                out=kh[:, sl, q2 * 1024:(q2 + 1) * 1024],
                                in_=dap(d_ccb_out, q2 * 128 * 32768 + 1024 * h,
                                        [[32768, 128], [1, 1024]])), sem=f"s_kh{sl}", n=16)
                            b.i("sy", lambda E, h=h, sl=sl, q2=q2: E.dma_start(
                                out=vh[:, sl, q2 * 1024:(q2 + 1) * 1024],
                                in_=dap(d_ccb_out, q2 * 128 * 32768 + 16384 + 1024 * h,
                                        [[32768, 128], [1, 1024]])),
                                sem=f"s_kh{sl}", n=16)
                        hload.append(b.cnt[f"s_kh{sl}"])
                        b.w("te", f"s_kh{sl}", hload[h])

                        jq = []
                        for m in range(2):
                            jmax = 8 * m + 8
                            if len(mden) >= 2:
                                b.w("te", "dve", mden[-2])
                            qs = sq[:, h * 1024 + 512 * m: h * 1024 + 512 * (m + 1)]
                            vden_prev = vden_last
                            act_gate = False

                            def emit_st(jt):
                                jsl = len(jq) % 3
                                koff = 1024 * (jt % 2) + 128 * (jt // 2)
                                kt = kh[:, sl, koff:koff + 128]
                                if len(jq) >= 3:
                                    b.w("te", jq[-3][1], jq[-3][0])
                                vst = b.i("te", lambda E, jsl=jsl, kt=kt, qs=qs:
                                          E.matmul(p_st[jsl][:, :], kt, qs,
                                                   start=True, stop=True), sem="te")
                                nonlocal act_gate
                                b.w("act", "te", vst)
                                if not act_gate:
                                    b.w("act", "te", vden_prev)
                                    act_gate = True
                                vex2 = b.i("act", lambda E, jsl=jsl, jt=jt:
                                           E.activation(pt[:, jt, :],
                                                        p_st[jsl][:, :],
                                                        AF.Exp, bias=SM_SHIFT,
                                                        scale=SM_SCALE), sem="act")
                                if jt >= 8 * m:
                                    kk = jt - 8 * m
                                    b.w("dve", "act", vex2)
                                    vmk = b.i("dve", lambda E, kk=kk, jt=jt:
                                              E.tensor_mul(pt[:, jt, :], pt[:, jt, :],
                                                           msk[:, kk * 512:(kk + 1) * 512]),
                                              sem="dve")
                                    jq.append((vmk, "dve"))
                                else:
                                    jq.append((vex2, "act"))

                            def emit_pv(jt, msl=m % 2, jmax=jmax):
                                koff = 1024 * (jt % 2) + 128 * (jt // 2)
                                vt = vh[:, sl, koff:koff + 128]
                                b.w("te", jq[base + jt][1], jq[base + jt][0])
                                b.i("te", lambda E, msl=msl, jt=jt, vt=vt, jmax=jmax:
                                    E.matmul(p_ov[msl][:, :], vt, pt[:, jt, :],
                                             start=(jt == 0), stop=(jt == jmax - 1)),
                                    sem=("te" if jt == jmax - 1 else None))

                            base = len(jq)
                            for jt in range(jmax):
                                emit_st(jt)
                                if jt >= 2:
                                    emit_pv(jt - 2)
                            emit_pv(jmax - 2)
                            emit_pv(jmax - 1)
                            msl = m % 2
                            b.w("te", "dve", vrr_last)
                            for jt in range(jmax):
                                vden_last = b.i(
                                    "te", lambda E, jt=jt, jmax=jmax:
                                    E.matmul(p_den[0:1, :],
                                             ones_cb[:, :], pt[:, jt, :],
                                             start=(jt == 0), stop=(jt == jmax - 1)),
                                    sem=("te" if jt == jmax - 1 else None))
                            vj = b.cnt["te"]
                            b.w("dve", "te", vj)
                            def _recipb(E, msl=msl):
                                with nc.allow_low_precision(reason="f32r recip row"):
                                    return E.reciprocal(rrow[0:1, msl * 512:(msl + 1) * 512],
                                                        p_den[0:1, :])
                            vrr_last = b.i("dve", _recipb, sem="dve")
                            b.w("te", "dve", vrr_last)
                            b.w("te", "dve", vrc_last)
                            vbc = b.i("te", lambda E, msl=msl:
                                      E.matmul(p_bc[:, :], ones_r[0:1, 0:128],
                                               rrow[0:1, msl * 512:(msl + 1) * 512],
                                               start=True, stop=True), sem="te")
                            b.w("dve", "te", vbc)
                            vrc_last = b.i("dve", lambda E, msl=msl:
                                           E.tensor_copy(rb_sb[:, msl * 512:(msl + 1) * 512],
                                                         p_bc[:, :]), sem="dve")
                            b.w("dve", "dve", vrc_last)
                            mden.append(b.i("dve", lambda E, msl=msl, h=h, m=m:
                                            E.scalar_tensor_tensor(
                                                sq[:, h * 1024 + 512 * m: h * 1024 + 512 * (m + 1)],
                                                p_ov[msl][:, :], F0,
                                                rb_sb[:, msl * 512:(msl + 1) * 512],
                                                ALU.bypass, ALU.mult), sem="dve"))
                        hb_te.append(b.cnt["te"])
                    vattn = b.cnt["dve"]

                with contextlib.ExitStack() as ps:
                    p_o = [ps.enter_context(nc.psum_tensor(f"pob{i}", [128, 512], F32))
                           for i in range(2)]
                    ob_te = []
                    o_ev = []
                    for oc in range(NCH):
                        sl = oc % 2
                        if oc >= 2:
                            b.w("gp", "te", ob_te[2 * (oc - 2) + 1])
                        else:
                            b.w("gp", "te", qb_te[2 * (14 + oc) + 1])
                        vld = b.i("gp", lambda E, oc=oc, sl=sl: E.dma_start(
                            out=qp[:, sl, :],
                            in_=dap(d_opt, oc * 128 * 2048, [[2048, 128], [1, 2048]])),
                            sem=f"s_qp{sl}", n=16)
                        b.w("te", f"s_qp{sl}", vld)
                        if oc == 0:
                            b.w("te", "dve", vattn)
                        for T in range(2):
                            gsl = (2 * oc + T) % 2
                            if len(o_ev) >= 2:
                                b.w("te", "dve", o_ev[-2])
                            vte = None
                            for c in range(NCH):
                                vte = b.i("te", lambda E, gsl=gsl, sl=sl, c=c, T=T:
                                          E.matmul(p_o[gsl][:, :],
                                                   qp[:, sl, c * 128:(c + 1) * 128],
                                                   sq[:, c * 1024 + T * 512: c * 1024 + (T + 1) * 512],
                                                   start=(c == 0), stop=(c == 15)),
                                          sem=("te" if c == 15 else None))
                            ob_te.append(vte)
                            b.w("dve", "te", vte)
                            o_ev.append(b.i("dve", lambda E, gsl=gsl, oc=oc, T=T:
                                            E.tensor_add(
                                                x_sb[:, oc * 1024 + T * 512: oc * 1024 + (T + 1) * 512],
                                                x_sb[:, oc * 1024 + T * 512: oc * 1024 + (T + 1) * 512],
                                                p_o[gsl][:, :]), sem="dve"))

        if upto >= 1:
            emit_attn_a()
            b.barrier()
        if upto >= 2:
            emit_ffn(1)
            b.barrier()
        if upto >= 3:
            emit_attn_b()
            b.barrier()
        if upto >= 4:
            emit_ffn(2)
            b.barrier()

        if upto >= 4:
            b.w("gp", "st", b.cnt["st"])
        else:
            b.w("sy", "dve", b.cnt["dve"])
            b.i("sy", lambda E: E.dma_start(
                out=dap(d_y, 0, [[1024, 128], [131072, 16], [1, 1024]]),
                in_=x_sb.ap()), sem="st", n=16)
            b.w("sy", "st", b.cnt["st"])

        with nc.Block() as block:
            b.replay(block)

    return nc



def _tile4(w):
    return np.ascontiguousarray(
        w.reshape(16, 128, 16, 128).transpose(0, 3, 2, 1).reshape(16, 128, 2048))


def _build_masks(p):
    import ml_dtypes
    out = np.ones((8, 128, 512), dtype=ml_dtypes.bfloat16)
    jr = np.arange(128)[:, None]
    ic = np.arange(128)[None, :]
    for k in range(8):
        for qq in range(4):
            delta = k - 2 * qq - p
            blk = out[k, :, qq * 128:(qq + 1) * 128]
            if delta == 0:
                blk[:] = np.where(jr > ic, 0.0, 1.0).astype(ml_dtypes.bfloat16)
            elif delta > 0:
                blk[:] = ml_dtypes.bfloat16(0.0)
            else:
                blk[:] = ml_dtypes.bfloat16(1.0)
    return out


def kernel(**inputs):
    x = np.ascontiguousarray(np.asarray(inputs["L0_attn_a_hidden_states"], dtype=np.float32))
    wq = np.asarray(inputs["L0_attn_a_Wq"], dtype=np.float32)
    bq = np.asarray(inputs["L0_attn_a_bq"], dtype=np.float32)
    wk = np.asarray(inputs["L0_attn_a_Wk"], dtype=np.float32)
    bk = np.asarray(inputs["L0_attn_a_bk"], dtype=np.float32)
    wv = np.asarray(inputs["L0_attn_a_Wv"], dtype=np.float32)
    bv = np.asarray(inputs["L0_attn_a_bv"], dtype=np.float32)
    wo = np.asarray(inputs["L0_attn_a_Wo"], dtype=np.float32)
    qp = np.asarray(inputs["L0_attn_b_q_proj"], dtype=np.float32)
    kp = np.asarray(inputs["L0_attn_b_k_proj"], dtype=np.float32)
    vp = np.asarray(inputs["L0_attn_b_v_proj"], dtype=np.float32)
    op = np.asarray(inputs["L0_attn_b_o_proj"], dtype=np.float32)
    wg = np.asarray(inputs["L0_ffn_w_gating"], dtype=np.float32)
    wl = np.asarray(inputs["L0_ffn_w_linear"], dtype=np.float32)

    shared = {
        "wqt": _tile4(wq),
        "wkvt": np.ascontiguousarray(np.concatenate([wk, wv], axis=0).T),
        "bq": np.ascontiguousarray(bq.reshape(1, 2048)),
        "bkv": np.ascontiguousarray(np.concatenate([bk, bv]).reshape(1, 1024)),
        "wot": _tile4(wo),
        "qpt": _tile4(qp),
        "kpt": _tile4(kp),
        "vpt": np.ascontiguousarray(vp.T),
        "opt": _tile4(op),
        "wgdr": np.ascontiguousarray(
            wg.reshape(2, 8, 2, 128, 64, 128).transpose(0, 4, 3, 1, 2, 5)
            .reshape(2, 64, 128, 2048)),
        "onesin": np.ones((128, 512), dtype=np.float32),
        "wldr": np.ascontiguousarray(
            wl.reshape(32, 2, 128, 16, 128).transpose(3, 2, 0, 1, 4)
            .reshape(16, 128, 8192)),
    }
    masks = [_build_masks(0), _build_masks(1)]

    in_maps = []
    for c in range(8):
        s, p = divmod(c, 2)
        rows = np.concatenate(
            [x[s, 128 * (2 * l + p):128 * (2 * l + p) + 128, :] for l in range(8)], axis=0)
        m = dict(shared)
        m["xfm"] = np.ascontiguousarray(rows.T)
        m["masks"] = masks[p]
        in_maps.append(m)

    nc = build_program(_UPTO)
    res = run_bass_kernel_spmd(nc, in_maps, core_ids=list(range(8)))

    y = np.empty((4, 2048, 2048), dtype=np.float32)
    for c in range(8):
        s, p = divmod(c, 2)
        tok = np.asarray(res.results[c]["yfm"]).T
        for l in range(8):
            y[s, 128 * (2 * l + p):128 * (2 * l + p) + 128, :] = tok[128 * l:128 * (l + 1), :]
    return y


# version 09 (final)
# speedup vs baseline: 1.0897x; 1.0897x over previous
import contextlib
import math
import numpy as np
import concourse.bass as bass
import concourse.mybir as mybir
from concourse.bass_utils import run_bass_kernel_spmd

F32 = mybir.dt.float32
F32R = mybir.dt.float32r
BF16 = mybir.dt.bfloat16
FP8 = mybir.dt.float8e4
AF = mybir.ActivationFunctionType
ALU = mybir.AluOpType
DRM = mybir.MatmulPerfMode.DoubleRow

D = 2048
TPC = 1024
NCH = 16
NT = 8
NH = 16
NEG = -1.0e30
SM_SCALE = np.float32(1.0 / math.sqrt(128.0))
SM_SHIFT = np.float32(-16.0)
LN_EPS = 1e-5
LA_EPS = 1e-6
INV_D = 1.0 / 2048.0
F0 = 0.0
FN1 = -1.0
FNH = np.float32(-0.5)

_UPTO = 4

PAIRS = [[0, 1], [2, 3], [4, 5], [6, 7]]


class Bld:
    ENGS = ("te", "act", "dve", "gp", "sy")

    def __init__(self, nc):
        self.nc = nc
        self.q = {e: [] for e in self.ENGS}
        self.cnt = {}
        self.sem = {}

    def mksem(self, stack, name):
        self.sem[name] = stack.enter_context(self.nc.semaphore(name))
        self.cnt[name] = 0

    def w(self, eng, name, val):
        if val is None or val <= 0:
            return
        s = self.sem[name]
        self.q[eng].append(lambda E, s=s, v=val: E.wait_ge(s, v))

    def i(self, eng, fn, sem=None, n=1):
        if sem is None and eng in ("dve", "act"):
            sem = eng
        if sem is None:
            self.q[eng].append(fn)
            return None
        v = self.cnt[sem] = self.cnt[sem] + n
        s = self.sem[sem]
        self.q[eng].append(lambda E, fn=fn, s=s, n=n: fn(E).then_inc(s, n))
        return v

    def dmabar(self, engs):
        if not hasattr(self, "lb_d"):
            self.lb_d = {}
        for s, v in self.cnt.items():
            if s.startswith("s_") and v > self.lb_d.get(s, 0):
                self.lb_d[s] = v
                for e in engs:
                    self.w(e, s, v)

    def qbar(self):
        if not hasattr(self, "last_bar"):
            self.last_bar = {}
        vals = {e: self.cnt[e] for e in ("te", "act", "dve")}
        for s, v in self.cnt.items():
            if s.startswith("s_") and v > self.last_bar.get(s, 0):
                vals[s] = v
                self.last_bar[s] = v
        for e in ("gp", "sy"):
            for o, v in vals.items():
                self.w(e, o, v)

    def barrier(self):
        if not hasattr(self, "last_bar"):
            self.last_bar = {}
        vals = {e: self.cnt[e] for e in ("te", "act", "dve")}
        for s, v in self.cnt.items():
            if s.startswith("s_") and v > self.last_bar.get(s, 0):
                vals[s] = v
                self.last_bar[s] = v
        for e in self.ENGS:
            for o, v in vals.items():
                self.w(e, o, v)

    def replay(self, block):
        b = self

        @block.tensor
        def _(E):
            for fn in b.q["te"]:
                fn(E)

        @block.scalar
        def _(E):
            for fn in b.q["act"]:
                fn(E)

        @block.vector
        def _(E):
            for fn in b.q["dve"]:
                fn(E)

        @block.gpsimd
        def _(E):
            for fn in b.q["gp"]:
                fn(E)

        @block.sync
        def _(E):
            for fn in b.q["sy"]:
                fn(E)


def build_program(upto=_UPTO):
    nc = bass.Bass("TRN2", target_bir_lowering=False)

    d_x = nc.dram_tensor("xfm", [D, TPC], F32R, kind="ExternalInput")
    d_wqt = nc.dram_tensor("wqt", [16, 128, 2048], F32, kind="ExternalInput")
    d_wkvt = nc.dram_tensor("wkvt", [D, 1024], F32, kind="ExternalInput")
    d_bq = nc.dram_tensor("bq", [1, D], F32, kind="ExternalInput")
    d_bkv = nc.dram_tensor("bkv", [1, 1024], F32, kind="ExternalInput")
    d_wot = nc.dram_tensor("wot", [16, 128, 2048], F32, kind="ExternalInput")
    d_qpt = nc.dram_tensor("qpt", [16, 128, 2048], F32, kind="ExternalInput")
    d_kpt = nc.dram_tensor("kpt", [16, 128, 2048], F32, kind="ExternalInput")
    d_vpt = nc.dram_tensor("vpt", [D, D], F32, kind="ExternalInput")
    d_opt = nc.dram_tensor("opt", [16, 128, 2048], F32, kind="ExternalInput")
    d_wgdr = nc.dram_tensor("wgdr", [2, 64, 128, 2048], F32, kind="ExternalInput")
    d_wldr = nc.dram_tensor("wldr", [16, 128, 8192], F32, kind="ExternalInput")
    d_masks = nc.dram_tensor("masks", [8, 128, 512], BF16, kind="ExternalInput")
    d_ones = nc.dram_tensor("onesin", [128, 512], F32, kind="ExternalInput")
    d_y = nc.dram_tensor("yfm", [D, TPC], F32R, kind="ExternalOutput")

    d_ccw_in = nc.dram_tensor("ccw_in", [1, 128], mybir.dt.int32)
    d_ccw_out = nc.dram_tensor("ccw_out", [2, 128], mybir.dt.int32)
    d_cca_in = nc.dram_tensor("cca_in", [128, 516], F32)
    d_cca_out = nc.dram_tensor("cca_out", [256, 516], F32)
    d_ccb_in = nc.dram_tensor("ccb_in", [128, 32768], BF16)
    d_ccb_out = nc.dram_tensor("ccb_out", [256, 32768], BF16)
    d_stg = nc.dram_tensor("stash_g", [2, 64, 128, 2048], FP8)
    d_stl = nc.dram_tensor("stash_l", [16, 128, 8192], FP8)

    def dap(t, off, ap):
        return bass.AP(t, off, ap)

    with contextlib.ExitStack() as top:
        b = Bld(nc)
        E = top.enter_context
        semnames = ["te", "act", "dve", "gpi", "st", "cc",
                    "bar_te", "bar_act", "bar_dve", "bar_gp", "bar_sy",
                    "s_xin", "s_ones", "s_bias", "s_wkv0", "s_wkv1",
                    "s_wq0", "s_wq1", "s_ccaex", "s_ccarb",
                    "s_qp0", "s_qp1", "s_vp", "s_kx0", "s_kx1",
                    "s_kh0", "s_kh1", "s_msk"]
        for call in (1, 2):
            for sl in (0, 1):
                semnames += [f"s_wg{call}{sl}", f"s_wl{call}{sl}"]
        semnames += ["s_stg0", "s_stg1", "s_stl0", "s_stl1"]
        for s in semnames:
            b.mksem(top, s)

        x_sb = E(nc.sbuf_tensor("x_sb", [128, NCH * 1024], F32R))
        ones_c = E(nc.sbuf_tensor("ones_c", [128, 1], F32R))
        ones_r = E(nc.sbuf_tensor("ones_r", [1, 512], F32R))
        ones_cb = E(nc.sbuf_tensor("ones_cb", [128, 1], BF16))
        ones_rb = E(nc.sbuf_tensor("ones_rb", [1, 512], BF16))
        st_sb = E(nc.sbuf_tensor("st_sb", [1, 3072], F32R))
        scr_sb = E(nc.sbuf_tensor("scr_sb", [128, 1024], F32R))
        rb_sb = E(nc.sbuf_tensor("rb_sb", [128, 1024], F32))
        bb_sb = E(nc.sbuf_tensor("bb_sb", [128, 2048], F32))
        et2_sb = E(nc.sbuf_tensor("et2_sb", [128, 1024], BF16))

        rA = st_sb[0:1, 0:1024]
        rB = st_sb[0:1, 1024:2048]
        rC = st_sb[0:1, 2048:3072]
        nrow = st_sb[0:1, 0:1024]
        rrow = st_sb[0:1, 1024:2048]

        b.i("sy", lambda E: E.dma_start(
            out=x_sb.ap(),
            in_=dap(d_x, 0, [[1024, 128], [131072, 16], [1, 1024]])),
            sem="s_xin", n=16)
        b.i("gp", lambda E: E.dma_start(out=ones_c[:, :], in_=d_ones[:, 0:1], allow_slow_non_contiguous=True), sem="s_ones", n=16)
        b.i("gp", lambda E: E.dma_start(out=ones_r[:, :], in_=d_ones[0:1, :]), sem="s_ones", n=16)
        b.i("gp", lambda E: E.dma_start(out=ones_cb[:, :], in_=d_ones[:, 0:1], allow_slow_non_contiguous=True), sem="s_ones", n=16)
        b.i("gp", lambda E: E.dma_start(out=ones_rb[:, :], in_=d_ones[0:1, :]), sem="s_ones", n=16)
        for e in ("te", "act", "dve"):
            b.w(e, "s_xin", 16)
            b.w(e, "s_ones", 64)
        b.i("gp", lambda E: E.collective_compute(
            "AllGather", ALU.bypass, replica_groups=PAIRS,
            ins=[d_ccw_in.ap().opt()], outs=[d_ccw_out.ap().opt()]),
            sem="cc", n=1)

        def emit_ln(tag, dst_kind, xs=None, xs8=None, t1h=None, pred=None):
            with contextlib.ExitStack() as ps:
                p_shift = ps.enter_context(nc.psum_tensor("psh" + tag, [128, 1024], F32))
                p_sx = ps.enter_context(nc.psum_tensor("psx" + tag, [1, 1024], F32))
                p_sxx = ps.enter_context(nc.psum_tensor("psxx" + tag, [1, 1024], F32))

                mm_done = []
                for k in range(32):
                    c, hf = divmod(k, 2)
                    sl = (k % 2) * 512
                    xh = x_sb[:, k * 512:(k + 1) * 512]
                    if pred is not None:
                        b.w("te", "dve", pred[k])
                        b.w("act", "dve", pred[k])
                    b.i("te", lambda E, hf=hf, xh=xh, c=c:
                        E.matmul(p_sx[0:1, hf * 512:(hf + 1) * 512], ones_c[:, :], xh,
                                 start=(c == 0), stop=(c == 15)),
                        sem=("te" if c == 15 else None))
                    if k >= 2:
                        b.w("act", "te", mm_done[k - 2])
                    vsq = b.i("act", lambda E, sl=sl, xh=xh:
                              E.activation(scr_sb[:, sl:sl + 512], xh, AF.Square),
                              sem="act")
                    b.w("te", "act", vsq)
                    mm_done.append(b.i("te", lambda E, hf=hf, sl=sl, c=c:
                                       E.matmul(p_sxx[0:1, hf * 512:(hf + 1) * 512],
                                                ones_c[:, :], scr_sb[:, sl:sl + 512],
                                                start=(c == 0), stop=(c == 15)),
                                       sem="te"))

                b.w("dve", "te", mm_done[-1])
                b.i("dve", lambda E: E.tensor_scalar_mul(rA, p_sx[0:1, :], INV_D))
                v2 = b.i("dve", lambda E: E.tensor_scalar_mul(rB, p_sxx[0:1, :], INV_D))
                b.w("dve", "dve", v2)
                v3 = b.i("dve", lambda E: E.scalar_tensor_tensor(
                    rC, rA, FN1, rA, ALU.mult, ALU.mult))
                b.w("dve", "dve", v3)
                vd = b.i("dve", lambda E: E.scalar_tensor_tensor(
                    rB, rB, LN_EPS, rC, ALU.add, ALU.add), sem="dve")
                b.w("act", "dve", vd)
                vL = b.i("act", lambda E: E.activation(rC, rB, AF.Ln))
                b.w("act", "act", vL)
                va = b.i("act", lambda E: E.activation(rB, rC, AF.Exp, scale=FNH),
                         sem="act")
                b.w("dve", "act", va)
                vn = b.i("dve", lambda E: E.tensor_scalar_mul(rC, rA, FN1),
                         sem="dve")

            b.dmabar(("te", "dve"))
            with contextlib.ExitStack() as ps:
                p_bA = ps.enter_context(nc.psum_tensor("pbA" + tag, [128, 1024], F32))
                p_bB = ps.enter_context(nc.psum_tensor("pbB" + tag, [128, 1024], F32))
                b.w("te", "act", va)
                b.w("te", "dve", vn)
                if pred is not None:
                    b.w("te", "dve", pred[-1])
                vb = None
                for hf in range(2):
                    b.i("te", lambda E, hf=hf:
                        E.matmul(p_bA[:, hf * 512:(hf + 1) * 512], ones_r[0:1, 0:128],
                                 rB[0:1, hf * 512:(hf + 1) * 512], start=True, stop=True),
                        sem="te")
                    vb = b.i("te", lambda E, hf=hf:
                             E.matmul(p_bB[:, hf * 512:(hf + 1) * 512], ones_r[0:1, 0:128],
                                      rC[0:1, hf * 512:(hf + 1) * 512], start=True, stop=True),
                             sem="te")
                b.w("dve", "te", vb)
                b.i("dve", lambda E: E.tensor_copy(bb_sb[:, 0:1024], p_bA[:, :]))
                vcp = b.i("dve", lambda E: E.tensor_copy(bb_sb[:, 1024:2048], p_bB[:, :]),
                          sem="dve")
                b.w("dve", "dve", vcp)
                b.w("te", "dve", vcp)
            aply = []
            for k in range(32):
                c, hf = divmod(k, 2)
                xh = x_sb[:, k * 512:(k + 1) * 512]
                bAh = bb_sb[:, hf * 512:(hf + 1) * 512]
                bBh = bb_sb[:, 1024 + hf * 512:1024 + (hf + 1) * 512]
                if dst_kind == "bf16":
                    dst = xs[:, k * 512:(k + 1) * 512]
                    v1 = b.i("dve", lambda E, dst=dst, xh=xh, bBh=bBh:
                             E.tensor_add(dst, xh, bBh))
                    b.w("dve", "dve", v1)
                    aply.append(b.i("dve", lambda E, dst=dst, bAh=bAh:
                                    E.tensor_mul(dst, dst, bAh), sem="dve"))
                else:
                    sl = (k % 2) * 512
                    dst = xs8[:, c // 2, c % 2, hf * 512:(hf + 1) * 512]
                    v1 = b.i("dve", lambda E, sl=sl, xh=xh, bBh=bBh:
                             E.tensor_add(t1h[:, sl:sl + 512], xh, bBh))
                    b.w("dve", "dve", v1)
                    aply.append(b.i("dve", lambda E, dst=dst, sl=sl, bAh=bAh:
                                    E.tensor_mul(dst, t1h[:, sl:sl + 512], bAh),
                                    sem="dve"))
            return aply

        def emit_attn_a(pred=None):
            with contextlib.ExitStack() as sb:
                xs = sb.enter_context(nc.sbuf_tensor("xsa", [128, NCH * 1024], BF16))
                sq = sb.enter_context(nc.sbuf_tensor("sqa", [128, NCH * 1024], BF16))
                wkv = sb.enter_context(nc.sbuf_tensor("wkva", [128, 2, 1024], BF16))
                wq = sb.enter_context(nc.sbuf_tensor("wqa", [128, 2, 2048], BF16))
                sk_sb = sb.enter_context(nc.sbuf_tensor("ska", [128, NT * 512], BF16))
                v_sb = sb.enter_context(nc.sbuf_tensor("va", [128, NT * 512], BF16))
                cca = sb.enter_context(nc.sbuf_tensor("cca", [128, 1032], F32))
                kvt = sb.enter_context(nc.sbuf_tensor("kvta", [128, 512], BF16))
                ksc = sb.enter_context(nc.sbuf_tensor("ksca", [128, 4], BF16))
                bq_sb = sb.enter_context(nc.sbuf_tensor("bqa", [1, 2048], BF16))
                bkv_sb = sb.enter_context(nc.sbuf_tensor("bkva", [1, 1024], BF16))

                aply = emit_ln("a1", "bf16", xs=xs, pred=pred)
                if pred is not None:
                    b.w("te", "dve", pred[-1])

                b.i("gp", lambda E: E.dma_start(out=bq_sb[:, :], in_=d_bq.ap()),
                    sem="s_bias", n=16)
                b.i("gp", lambda E: E.dma_start(out=bkv_sb[:, :], in_=d_bkv.ap()),
                    sem="s_bias", n=16)
                b.w("te", "s_bias", 32)

                with contextlib.ExitStack() as ps:
                    p_kv = [ps.enter_context(nc.psum_tensor(f"pkv{i}", [128, 1024], F32))
                            for i in range(4)]
                    ld = []
                    for rep in range(2):
                        for c in range(NCH):
                            k = rep * NCH + c
                            sl = k % 2
                            if k >= 2:
                                b.w("gp", "te", ld[k - 2][1])
                            vld = b.i("gp", lambda E, c=c, sl=sl: E.dma_start(
                                out=wkv[:, sl, :],
                                in_=dap(d_wkvt, c * 131072, [[1024, 128], [1, 1024]])),
                                sem=f"s_wkv{sl}", n=16)
                            b.w("te", f"s_wkv{sl}", vld)
                            if rep == 0:
                                b.w("te", "dve", aply[2 * c + 1])
                            vte = None
                            for tt4 in range(4):
                                tt = rep * 4 + tt4
                                lhs = xs[:, c * 1024 + tt * 128: c * 1024 + tt * 128 + 128]
                                for hf in range(2):
                                    vte = b.i("te", lambda E, tt4=tt4, sl=sl, hf=hf, c=c, lhs=lhs:
                                              E.matmul(p_kv[tt4][:, hf * 512:(hf + 1) * 512],
                                                       lhs,
                                                       wkv[:, sl, hf * 512:(hf + 1) * 512],
                                                       start=(c == 0), stop=False),
                                              sem=("te" if (tt4 == 3 and hf == 1) else None))
                            ld.append((vld, vte))
                        for tt4 in range(4):
                            tt = rep * 4 + tt4
                            vte = None
                            for hf in range(2):
                                vte = b.i("te", lambda E, tt4=tt4, hf=hf:
                                          E.matmul(p_kv[tt4][:, hf * 512:(hf + 1) * 512],
                                                   ones_rb[0:1, 0:128],
                                                   bkv_sb[0:1, hf * 512:(hf + 1) * 512],
                                                   start=False, stop=True),
                                          sem="te")
                            esl = (tt % 2) * 512
                            b.w("dve", "te", vte)
                            vmin = b.i("dve", lambda E, tt4=tt4, esl=esl:
                                       E.tensor_scalar_min(scr_sb[:, esl:esl + 512],
                                                           p_kv[tt4][:, 0:512], F0),
                                       sem="dve")
                            b.w("act", "dve", vmin)
                            vexp = b.i("act", lambda E, esl=esl:
                                       E.activation(et2_sb[:, esl:esl + 512],
                                                    scr_sb[:, esl:esl + 512], AF.Exp),
                                       sem="act")
                            b.w("act", "te", vte)
                            vvc = b.i("act", lambda E, tt4=tt4, tt=tt:
                                      E.copy(v_sb[:, tt * 512:(tt + 1) * 512],
                                             p_kv[tt4][:, 512:1024]), sem="act")
                            b.w("dve", "act", vexp)
                            vsk = b.i("dve", lambda E, tt4=tt4, tt=tt, esl=esl:
                                      E.scalar_tensor_tensor(
                                          sk_sb[:, tt * 512:(tt + 1) * 512],
                                          p_kv[tt4][:, 0:512], F0,
                                          et2_sb[:, esl:esl + 512],
                                          ALU.max, ALU.add), sem="dve")
                        if rep == 0:
                            b.w("te", "dve", b.cnt["dve"])
                            b.w("te", "act", b.cnt["act"])
                    vkv_done = (b.cnt["dve"], b.cnt["act"])

                with contextlib.ExitStack() as ps:
                    p_kk = ps.enter_context(nc.psum_tensor("pkk", [128, 1024], F32))
                    b.w("te", "dve", vkv_done[0])
                    b.w("te", "act", vkv_done[1])
                    vkk = None
                    for g in range(4):
                        for tt in range(NT):
                            lhs = sk_sb[:, tt * 512 + g * 128: tt * 512 + (g + 1) * 128]
                            b.i("te", lambda E, g=g, tt=tt, lhs=lhs:
                                E.matmul(p_kk[:, g * 128:(g + 1) * 128], lhs,
                                         v_sb[:, tt * 512 + g * 128: tt * 512 + (g + 1) * 128],
                                         start=(tt == 0), stop=(tt == 7)),
                                sem=("te" if tt == 7 else None))
                        for tt in range(NT):
                            lhs = sk_sb[:, tt * 512 + g * 128: tt * 512 + (g + 1) * 128]
                            vkk = b.i("te", lambda E, g=g, tt=tt, lhs=lhs:
                                      E.matmul(p_kk[:, 512 + g: 513 + g], lhs,
                                               ones_cb[:, :],
                                               start=(tt == 0), stop=(tt == 7)),
                                      sem="te")
                    b.w("dve", "te", vkk)
                    vkevac = b.i("dve", lambda E:
                                 E.tensor_copy(cca[:, 0:516], p_kk[:, 0:516]),
                                 sem="dve")

                b.w("gp", "dve", vkevac)
                vex = b.i("gp", lambda E: E.dma_start(out=d_cca_in.ap(),
                                                      in_=cca[:, 0:516]),
                          sem="s_ccaex", n=16)
                b.w("gp", "s_ccaex", vex)
                vcc = b.i("gp", lambda E: E.collective_compute(
                    "AllGather", ALU.bypass, replica_groups=PAIRS,
                    ins=[d_cca_in.ap().opt()], outs=[d_cca_out.ap().opt()]),
                    sem="cc", n=1)

                with contextlib.ExitStack() as ps:
                    p_q = [ps.enter_context(nc.psum_tensor(f"pq{i}", [128, 512], F32))
                           for i in range(2)]
                    b.w("te", "dve", vkevac)
                    q_te = []
                    q_ev = []
                    for h in range(NH):
                        sl = h % 2
                        if h >= 2:
                            b.w("gp", "te", q_te[2 * (h - 2) + 1])
                        vld = b.i("gp", lambda E, h=h, sl=sl: E.dma_start(
                            out=wq[:, sl, :],
                            in_=dap(d_wqt, h * 128 * 2048, [[2048, 128], [1, 2048]])),
                            sem=f"s_wq{sl}", n=16)
                        b.w("te", f"s_wq{sl}", vld)
                        for T in range(2):
                            gsl = (2 * h + T) % 2
                            if len(q_ev) >= 2:
                                b.w("te", "dve", q_ev[-2])
                            for c in range(NCH):
                                b.i("te", lambda E, gsl=gsl, sl=sl, c=c, T=T:
                                    E.matmul(p_q[gsl][:, :],
                                             wq[:, sl, c * 128:(c + 1) * 128],
                                             xs[:, c * 1024 + T * 512: c * 1024 + (T + 1) * 512],
                                             start=(c == 0), stop=False))
                            vq = b.i("te", lambda E, gsl=gsl, h=h:
                                     E.matmul(p_q[gsl][:, :], bq_sb[0:1, h * 128:(h + 1) * 128],
                                              ones_rb[0:1, 0:512], start=False, stop=True),
                                     sem="te")
                            q_te.append(vq)
                            esl = gsl * 512
                            b.w("dve", "te", vq)
                            vmin = b.i("dve", lambda E, gsl=gsl, esl=esl:
                                       E.tensor_scalar_min(scr_sb[:, esl:esl + 512],
                                                           p_q[gsl][:, :], F0), sem="dve")
                            b.w("act", "dve", vmin)
                            vexp = b.i("act", lambda E, esl=esl:
                                       E.activation(et2_sb[:, esl:esl + 512],
                                                    scr_sb[:, esl:esl + 512], AF.Exp),
                                       sem="act")
                            b.w("dve", "act", vexp)
                            q_ev.append(b.i("dve", lambda E, gsl=gsl, esl=esl, h=h, T=T:
                                            E.scalar_tensor_tensor(
                                                sq[:, h * 1024 + T * 512: h * 1024 + (T + 1) * 512],
                                                p_q[gsl][:, :], F0, et2_sb[:, esl:esl + 512],
                                                ALU.max, ALU.add), sem="dve"))
                    vq_all = q_ev[-1]

                b.w("gp", "cc", vcc)
                b.i("gp", lambda E: E.dma_start(out=cca[:, 0:516],
                                                in_=d_cca_out[0:128, :]),
                    sem="s_ccarb", n=16)
                vrb = b.i("gp", lambda E: E.dma_start(out=cca[:, 516:1032],
                                                      in_=d_cca_out[128:256, :]),
                          sem="s_ccarb", n=16)
                b.w("dve", "s_ccarb", vrb)
                vadd = b.i("dve", lambda E: E.tensor_add(cca[:, 0:516], cca[:, 0:516],
                                                         cca[:, 516:1032]))
                b.w("dve", "dve", vadd)
                b.i("dve", lambda E: E.tensor_copy(kvt[:, :], cca[:, 0:512]))
                vtot = b.i("dve", lambda E: E.tensor_copy(ksc[:, :], cca[:, 512:516]),
                           sem="dve")

                with contextlib.ExitStack() as ps:
                    p_qk = [ps.enter_context(nc.psum_tensor(f"pqk{i}", [128, 1024], F32))
                            for i in range(2)]
                    p_bc = ps.enter_context(nc.psum_tensor("pbca", [128, 1024], F32))
                    p_n = ps.enter_context(nc.psum_tensor("pn", [1, 1024], F32))
                    b.w("te", "dve", vtot)
                    b.w("te", "dve", vq_all)
                    vnr_last = 0
                    vcp_last = 0
                    sc_done = []
                    for h in range(NH):
                        g = h // 4
                        qsl = h % 2
                        b.w("te", "dve", vnr_last)
                        vn = None
                        for T in range(2):
                            vn = b.i("te", lambda E, T=T, g=g, h=h:
                                     E.matmul(p_n[0:1, T * 512:(T + 1) * 512],
                                              ksc[:, g:g + 1],
                                              sq[:, h * 1024 + T * 512: h * 1024 + (T + 1) * 512],
                                              start=True, stop=True), sem="te")
                        if len(sc_done) >= 2:
                            b.w("te", "dve", sc_done[-2])
                        vqk = None
                        for T in range(2):
                            vqk = b.i("te", lambda E, T=T, g=g, h=h, qsl=qsl:
                                      E.matmul(p_qk[qsl][:, T * 512:(T + 1) * 512],
                                               kvt[:, g * 128:(g + 1) * 128],
                                               sq[:, h * 1024 + T * 512: h * 1024 + (T + 1) * 512],
                                               start=True, stop=True), sem="te")
                        b.w("dve", "te", vn)
                        vna = b.i("dve", lambda E:
                                  E.tensor_scalar_add(nrow[0:1, 0:1024],
                                                      p_n[0:1, :], LA_EPS))
                        b.w("dve", "dve", vna)

                        def _recip(E):
                            with nc.allow_low_precision(reason="f32r recip row"):
                                return E.reciprocal(rrow[0:1, 0:1024], nrow[0:1, 0:1024])
                        vnr_last = b.i("dve", _recip, sem="dve")
                        b.w("te", "dve", vnr_last)
                        b.w("te", "dve", vcp_last)
                        vbc = None
                        for T in range(2):
                            vbc = b.i("te", lambda E, T=T:
                                      E.matmul(p_bc[:, T * 512:(T + 1) * 512],
                                               ones_r[0:1, 0:128],
                                               rrow[0:1, T * 512:(T + 1) * 512],
                                               start=True, stop=True), sem="te")
                        b.w("dve", "te", vbc)
                        b.w("dve", "te", vqk)
                        vcp_last = b.i("dve", lambda E:
                                       E.tensor_copy(rb_sb[:, 0:1024], p_bc[:, :]),
                                       sem="dve")
                        b.w("dve", "dve", vcp_last)
                        sc_done.append(b.i("dve", lambda E, h=h, qsl=qsl:
                                           E.scalar_tensor_tensor(
                                               sq[:, h * 1024: (h + 1) * 1024],
                                               p_qk[qsl][:, :], F0,
                                               rb_sb[:, 0:1024],
                                               ALU.bypass, ALU.mult), sem="dve"))
                    vattn = sc_done[-1]

                with contextlib.ExitStack() as ps:
                    p_o = [ps.enter_context(nc.psum_tensor(f"poa{i}", [128, 512], F32))
                           for i in range(2)]
                    o_te = []
                    o_ev = []
                    for oc in range(NCH):
                        sl = oc % 2
                        if oc >= 2:
                            b.w("gp", "te", o_te[2 * (oc - 2) + 1])
                        else:
                            b.w("gp", "te", q_te[2 * (14 + oc) + 1])
                        vld = b.i("gp", lambda E, oc=oc, sl=sl: E.dma_start(
                            out=wq[:, sl, :],
                            in_=dap(d_wot, oc * 128 * 2048, [[2048, 128], [1, 2048]])),
                            sem=f"s_wq{sl}", n=16)
                        b.w("te", f"s_wq{sl}", vld)
                        if oc == 0:
                            b.w("te", "dve", vattn)
                        for T in range(2):
                            gsl = (2 * oc + T) % 2
                            if len(o_ev) >= 2:
                                b.w("te", "dve", o_ev[-2])
                            vwo = None
                            for c in range(NCH):
                                vwo = b.i("te", lambda E, gsl=gsl, sl=sl, c=c, T=T:
                                          E.matmul(p_o[gsl][:, :],
                                                   wq[:, sl, c * 128:(c + 1) * 128],
                                                   sq[:, c * 1024 + T * 512: c * 1024 + (T + 1) * 512],
                                                   start=(c == 0), stop=(c == 15)),
                                          sem=("te" if c == 15 else None))
                            o_te.append(vwo)
                            b.w("dve", "te", vwo)
                            o_ev.append(b.i("dve", lambda E, gsl=gsl, oc=oc, T=T:
                                            E.tensor_add(
                                                x_sb[:, oc * 1024 + T * 512: oc * 1024 + (T + 1) * 512],
                                                x_sb[:, oc * 1024 + T * 512: oc * 1024 + (T + 1) * 512],
                                                p_o[gsl][:, :]), sem="dve"))
                return o_ev

        def emit_ffn(call, pred=None):
            tag = f"f{call}"
            with contextlib.ExitStack() as sb:
                xs8 = sb.enter_context(nc.sbuf_tensor("xs8" + tag, [128, 8, 2, 1024], FP8))
                t1h = sb.enter_context(nc.sbuf_tensor("t1h" + tag, [128, 1024], F32))
                act8 = sb.enter_context(nc.sbuf_tensor("act8" + tag, [128, 32, 2, 1024], FP8))
                wg0 = sb.enter_context(nc.sbuf_tensor("wg0" + tag, [128, 2, 8, 2, 128], FP8))
                wg1 = sb.enter_context(nc.sbuf_tensor("wg1" + tag, [128, 2, 8, 2, 128], FP8))
                wl = sb.enter_context(nc.sbuf_tensor("wl" + tag, [128, 2, 32, 2, 128], FP8))
                gtmp = sb.enter_context(nc.sbuf_tensor("gt" + tag, [128, 1024], BF16))

                aply = emit_ln(tag, "fp8", xs8=xs8, t1h=t1h, pred=pred)
                if pred is not None:
                    b.w("te", "dve", pred[-1])

                with contextlib.ExitStack() as ps:
                    p_g = [ps.enter_context(nc.psum_tensor(f"pg{i}" + tag, [128, 512], F32))
                           for i in range(2)]
                    p_f = [ps.enter_context(nc.psum_tensor(f"pf{i}" + tag, [128, 512], F32))
                           for i in range(2)]

                    g_te = []
                    g_st = []
                    g_act = []
                    g_dve = []
                    for hc in range(64):
                        sl = hc % 2
                        wsem = f"s_wg{call}{sl}"
                        if hc >= 2:
                            b.w("gp" if call == 1 else "sy", "te", g_te[hc - 2])
                            if call == 1:
                                b.w("gp", f"s_stg{sl}", g_st[hc - 2])
                        w0ap = dap(wg0, sl * 2048, [[4096, 128], [1, 2048]])
                        w1ap = dap(wg1, sl * 2048, [[4096, 128], [1, 2048]])
                        if call == 1:
                            b.i("gp", lambda E, hc=hc, w0ap=w0ap: E.dma_start(
                                out=w0ap,
                                in_=dap(d_wgdr, hc * 128 * 2048, [[2048, 128], [1, 2048]])),
                                sem=wsem, n=16)
                            vld = b.i("gp", lambda E, hc=hc, w1ap=w1ap: E.dma_start(
                                out=w1ap,
                                in_=dap(d_wgdr, (64 + hc) * 128 * 2048,
                                        [[2048, 128], [1, 2048]])),
                                sem=wsem, n=16)
                            b.w("sy", wsem, vld)
                            b.i("sy", lambda E, hc=hc, w0ap=w0ap: E.dma_start(
                                out=dap(d_stg, hc * 128 * 2048, [[2048, 128], [1, 2048]]),
                                in_=w0ap), sem=f"s_stg{sl}", n=16)
                            g_st.append(b.i("sy", lambda E, hc=hc, w1ap=w1ap: E.dma_start(
                                out=dap(d_stg, (64 + hc) * 128 * 2048,
                                        [[2048, 128], [1, 2048]]),
                                in_=w1ap), sem=f"s_stg{sl}", n=16))
                            b.w("te", wsem, vld)
                        else:
                            b.i("sy", lambda E, hc=hc, w0ap=w0ap: E.dma_start(
                                out=w0ap,
                                in_=dap(d_stg, hc * 128 * 2048, [[2048, 128], [1, 2048]])),
                                sem=wsem, n=16)
                            vld = b.i("sy", lambda E, hc=hc, w1ap=w1ap: E.dma_start(
                                out=w1ap,
                                in_=dap(d_stg, (64 + hc) * 128 * 2048,
                                        [[2048, 128], [1, 2048]])),
                                sem=wsem, n=16)
                            b.w("te", wsem, vld)

                        vte = None
                        for T in range(2):
                            gsl = (2 * hc + T) % 2
                            if len(g_dve) >= 2:
                                b.w("te", "dve", g_dve[-2])
                                b.w("te", "act", g_act[-2])
                            for i in range(8):
                                if hc == 0 and T == 0:
                                    b.w("te", "dve", aply[4 * i + 3])
                                b.i("te", lambda E, gsl=gsl, sl=sl, i=i, T=T:
                                    E.matmul(p_g[gsl][:, :], wg0[:, sl, i, :, :],
                                             xs8[:, i, :, T * 512:(T + 1) * 512],
                                             start=(i == 0), stop=(i == 7),
                                             perf_mode=DRM),
                                    sem=("te" if i == 7 else None))
                            for i in range(8):
                                vte = b.i("te", lambda E, gsl=gsl, sl=sl, i=i, T=T:
                                          E.matmul(p_f[gsl][:, :], wg1[:, sl, i, :, :],
                                                   xs8[:, i, :, T * 512:(T + 1) * 512],
                                                   start=(i == 0), stop=(i == 7),
                                                   perf_mode=DRM),
                                          sem=("te" if i == 7 else None))
                            b.w("act", "te", vte)
                            g_act.append(b.i("act", lambda E, gsl=gsl:
                                             E.activation(gtmp[:, gsl * 512:(gsl + 1) * 512],
                                                          p_g[gsl][:, :], AF.Gelu_apprx_tanh),
                                             sem="act"))
                            b.w("dve", "act", g_act[-1])
                            g_dve.append(b.i("dve", lambda E, gsl=gsl, hc=hc, T=T:
                                             E.tensor_mul(
                                                 act8[:, hc // 2, hc % 2, T * 512:(T + 1) * 512],
                                                 gtmp[:, gsl * 512:(gsl + 1) * 512],
                                                 p_f[gsl][:, :]), sem="dve"))
                        g_te.append(vte)
                    v_act8 = g_dve[-1]

                with contextlib.ExitStack() as ps:
                    p_o = [ps.enter_context(nc.psum_tensor(f"pfo{i}" + tag, [128, 512], F32))
                           for i in range(2)]
                    o_te = []
                    o_st = []
                    o_dve = []
                    for oc in range(NCH):
                        sl = oc % 2
                        lsem = f"s_wl{call}{sl}"
                        if oc >= 2:
                            b.w("gp" if call == 1 else "sy", "te", o_te[2 * (oc - 2) + 1])
                            if call == 1:
                                b.w("gp", f"s_stl{sl}", o_st[oc - 2])
                        wlap = dap(wl, sl * 8192, [[16384, 128], [1, 8192]])
                        if call == 1:
                            vld = b.i("gp", lambda E, oc=oc, wlap=wlap: E.dma_start(
                                out=wlap,
                                in_=dap(d_wldr, oc * 128 * 8192, [[8192, 128], [1, 8192]])),
                                sem=lsem, n=16)
                            b.w("sy", lsem, vld)
                            o_st.append(b.i("sy", lambda E, oc=oc, wlap=wlap: E.dma_start(
                                out=dap(d_stl, oc * 128 * 8192, [[8192, 128], [1, 8192]]),
                                in_=wlap), sem=f"s_stl{sl}", n=16))
                            b.w("te", lsem, vld)
                        else:
                            vld = b.i("sy", lambda E, oc=oc, wlap=wlap: E.dma_start(
                                out=wlap,
                                in_=dap(d_stl, oc * 128 * 8192, [[8192, 128], [1, 8192]])),
                                sem=lsem, n=16)
                            b.w("te", lsem, vld)
                        if oc == 0:
                            b.w("te", "dve", v_act8)
                        for T in range(2):
                            gsl = (2 * oc + T) % 2
                            if len(o_dve) >= 2:
                                b.w("te", "dve", o_dve[-2])
                            vte = None
                            for i in range(32):
                                vte = b.i("te", lambda E, gsl=gsl, sl=sl, i=i, T=T:
                                          E.matmul(p_o[gsl][:, :], wl[:, sl, i, :, :],
                                                   act8[:, i, :, T * 512:(T + 1) * 512],
                                                   start=(i == 0), stop=(i == 31),
                                                   perf_mode=DRM),
                                          sem=("te" if i == 31 else None))
                            o_te.append(vte)
                            b.w("dve", "te", vte)
                            o_dve.append(b.i("dve", lambda E, gsl=gsl, oc=oc, T=T:
                                             E.tensor_add(
                                                 x_sb[:, oc * 1024 + T * 512: oc * 1024 + (T + 1) * 512],
                                                 x_sb[:, oc * 1024 + T * 512: oc * 1024 + (T + 1) * 512],
                                                 p_o[gsl][:, :]), sem="dve"))
                            if call == 2:
                                b.w("gp", "dve", o_dve[-1])
                                b.i("gp", lambda E, oc=oc, T=T: E.dma_start(
                                    out=dap(d_y, oc * 128 * 1024 + T * 512,
                                            [[1024, 128], [1, 512]]),
                                    in_=x_sb[:, oc * 1024 + T * 512: oc * 1024 + (T + 1) * 512]),
                                    sem="st", n=16)
            return o_dve

        def emit_attn_b(pred=None):
            with contextlib.ExitStack() as sb:
                sq = sb.enter_context(nc.sbuf_tensor("sqb", [128, NCH * 1024], BF16))
                qp = sb.enter_context(nc.sbuf_tensor("qpb", [128, 2, 2048], BF16))
                kh = sb.enter_context(nc.sbuf_tensor("khb", [128, 2, 2048], BF16))
                vh = sb.enter_context(nc.sbuf_tensor("vhb", [128, 2, 2048], BF16))
                sb2 = sb.enter_context(contextlib.ExitStack())
                xs = sb2.enter_context(nc.sbuf_tensor("xsb", [128, NCH * 1024], BF16))
                vps = sb2.enter_context(nc.sbuf_tensor("vpb", [128, NCH * 512], BF16))
                ktmp = sb2.enter_context(nc.sbuf_tensor("ktb", [128, 2, 512], BF16))

                aply = emit_ln("a3", "bf16", xs=xs, pred=pred)
                if pred is not None:
                    b.w("te", "dve", pred[-1])

                with contextlib.ExitStack() as ps:
                    p_p = [ps.enter_context(nc.psum_tensor(f"pbp{i}", [128, 512], F32))
                           for i in range(2)]
                    kv_ev = []
                    kx = []

                    kb_te = []
                    for oc in range(NCH):
                        sl = oc % 2
                        if oc >= 2:
                            b.w("gp", "te", kb_te[2 * (oc - 2) + 1])
                        vld = b.i("gp", lambda E, oc=oc, sl=sl: E.dma_start(
                            out=qp[:, sl, :],
                            in_=dap(d_kpt, oc * 128 * 2048, [[2048, 128], [1, 2048]])),
                            sem=f"s_qp{sl}", n=16)
                        b.w("te", f"s_qp{sl}", vld)
                        for T in range(2):
                            gsl = (2 * oc + T) % 2
                            if len(kv_ev) >= 2:
                                b.w("te", "act", kv_ev[-2])
                            vte = None
                            for c in range(NCH):
                                if oc == 0 and T == 0:
                                    b.w("te", "dve", aply[2 * c + 1])
                                vte = b.i("te", lambda E, gsl=gsl, sl=sl, c=c, T=T:
                                          E.matmul(p_p[gsl][:, :],
                                                   qp[:, sl, c * 128:(c + 1) * 128],
                                                   xs[:, c * 1024 + T * 512: c * 1024 + (T + 1) * 512],
                                                   start=(c == 0), stop=(c == 15)),
                                          sem=("te" if c == 15 else None))
                            kb_te.append(vte)
                            b.w("act", "te", vte)
                            if len(kx) >= 2:
                                b.w("act", f"s_kx{gsl}", kx[-2])
                            kv_ev.append(b.i("act", lambda E, gsl=gsl:
                                             E.copy(ktmp[:, gsl, :], p_p[gsl][:, :]), sem="act"))
                            b.w("sy", "act", kv_ev[-1])
                            kx.append(b.i("sy", lambda E, gsl=gsl, oc=oc, T=T: E.dma_start(
                                out=d_ccb_in[:, oc * 1024 + T * 512: oc * 1024 + (T + 1) * 512],
                                in_=ktmp[:, gsl, :]), sem=f"s_kx{gsl}", n=16))

                    for og in range(4):
                        if og > 0:
                            b.w("gp", "te", b.cnt["te"])
                        vld = b.i("gp", lambda E, og=og: E.dma_start(
                            out=vps.ap(),
                            in_=dap(d_vpt, og * 512,
                                    [[2048, 128], [262144, 16], [1, 512]])),
                            sem="s_vp", n=16)
                        b.w("te", "s_vp", vld)
                        for tt in range(NT):
                            gsl = (4 * og + tt) % 2
                            if len(kv_ev) >= 2:
                                b.w("te", "act", kv_ev[-2])
                            vte = None
                            for c in range(NCH):
                                lhs = xs[:, c * 1024 + tt * 128: c * 1024 + tt * 128 + 128]
                                vte = b.i("te", lambda E, gsl=gsl, c=c, lhs=lhs:
                                          E.matmul(p_p[gsl][:, :], lhs,
                                                   vps[:, c * 512:(c + 1) * 512],
                                                   start=(c == 0), stop=(c == 15)),
                                          sem=("te" if c == 15 else None))
                            b.w("act", "te", vte)
                            if len(kx) >= 2:
                                b.w("act", f"s_kx{gsl}", kx[-2])
                            kv_ev.append(b.i("act", lambda E, gsl=gsl:
                                             E.copy(ktmp[:, gsl, :], p_p[gsl][:, :]), sem="act"))
                            b.w("sy", "act", kv_ev[-1])
                            kx.append(b.i("sy", lambda E, gsl=gsl, og=og, tt=tt: E.dma_start(
                                out=dap(d_ccb_in, 16384 + 1024 * (4 * og) + 128 * tt,
                                        [[32768, 128], [1024, 4], [1, 128]]),
                                in_=ktmp[:, gsl, :]), sem=f"s_kx{gsl}", n=16))

                    b.w("gp", "s_kx0", b.cnt["s_kx0"])
                    b.w("gp", "s_kx1", b.cnt["s_kx1"])
                    vccb = b.i("gp", lambda E: E.collective_compute(
                        "AllGather", ALU.bypass, replica_groups=PAIRS,
                        ins=[d_ccb_in.ap().opt()], outs=[d_ccb_out.ap().opt()]),
                        sem="cc", n=1)

                    qb_te = []
                    q_ev = []
                    for oc in range(NCH):
                        sl = oc % 2
                        if oc >= 2:
                            b.w("gp", "te", qb_te[2 * (oc - 2) + 1])
                        else:
                            b.w("gp", "te", kb_te[2 * (14 + oc) + 1])
                        vld = b.i("gp", lambda E, oc=oc, sl=sl: E.dma_start(
                            out=qp[:, sl, :],
                            in_=dap(d_qpt, oc * 128 * 2048, [[2048, 128], [1, 2048]])),
                            sem=f"s_qp{sl}", n=16)
                        b.w("te", f"s_qp{sl}", vld)
                        for T in range(2):
                            gsl = (2 * oc + T) % 2
                            if len(q_ev) >= 2:
                                b.w("te", "act", q_ev[-2])
                            vte = None
                            for c in range(NCH):
                                vte = b.i("te", lambda E, gsl=gsl, sl=sl, c=c, T=T:
                                          E.matmul(p_p[gsl][:, :],
                                                   qp[:, sl, c * 128:(c + 1) * 128],
                                                   xs[:, c * 1024 + T * 512: c * 1024 + (T + 1) * 512],
                                                   start=(c == 0), stop=(c == 15)),
                                          sem=("te" if c == 15 else None))
                            qb_te.append(vte)
                            b.w("act", "te", vte)
                            q_ev.append(b.i("act", lambda E, gsl=gsl, oc=oc, T=T:
                                            E.copy(sq[:, oc * 1024 + T * 512: oc * 1024 + (T + 1) * 512],
                                                   p_p[gsl][:, :]), sem="act"))
                    vq_all = q_ev[-1]

                sb2.close()
                pt = sb.enter_context(nc.sbuf_tensor("ptb", [128, 16, 512], BF16))
                msk = sb.enter_context(nc.sbuf_tensor("mskb", [128, 8 * 512], BF16))
                b.w("sy", "s_kx0", b.cnt["s_kx0"])
                b.w("sy", "s_kx1", b.cnt["s_kx1"])
                b.w("sy", "te", b.cnt["te"])
                vmask = b.i("sy", lambda E: E.dma_start(
                    out=msk.ap(),
                    in_=dap(d_masks, 0, [[512, 128], [65536, 8], [1, 512]])),
                    sem="s_msk", n=16)

                with contextlib.ExitStack() as ps:
                    p_st = [ps.enter_context(nc.psum_tensor(f"pst{i}", [128, 512], F32))
                            for i in range(3)]
                    p_ov = [ps.enter_context(nc.psum_tensor(f"pov{i}", [128, 512], F32))
                            for i in range(2)]
                    p_den = ps.enter_context(nc.psum_tensor("pden", [1, 512], F32))
                    p_bc = ps.enter_context(nc.psum_tensor("pbcb", [128, 512], F32))

                    b.w("te", "act", vq_all)
                    b.w("dve", "s_msk", vmask)
                    hb_te = []
                    hload = []
                    vden_last = 0
                    vrr_last = 0
                    vrc_last = 0
                    mden = []
                    for h in range(NH):
                        sl = h % 2
                        if h == 0:
                            b.w("sy", "cc", vccb)
                        if h >= 2:
                            b.w("sy", "te", hb_te[h - 2])
                        for q2 in range(2):
                            b.i("sy", lambda E, h=h, sl=sl, q2=q2: E.dma_start(
                                out=kh[:, sl, q2 * 1024:(q2 + 1) * 1024],
                                in_=dap(d_ccb_out, q2 * 128 * 32768 + 1024 * h,
                                        [[32768, 128], [1, 1024]])), sem=f"s_kh{sl}", n=16)
                            b.i("sy", lambda E, h=h, sl=sl, q2=q2: E.dma_start(
                                out=vh[:, sl, q2 * 1024:(q2 + 1) * 1024],
                                in_=dap(d_ccb_out, q2 * 128 * 32768 + 16384 + 1024 * h,
                                        [[32768, 128], [1, 1024]])),
                                sem=f"s_kh{sl}", n=16)
                        hload.append(b.cnt[f"s_kh{sl}"])
                        b.w("te", f"s_kh{sl}", hload[h])

                        jq = []
                        for m in range(2):
                            jmax = 8 * m + 8
                            if len(mden) >= 2:
                                b.w("te", "dve", mden[-2])
                            qs = sq[:, h * 1024 + 512 * m: h * 1024 + 512 * (m + 1)]
                            vden_prev = vden_last
                            act_gate = False

                            def emit_st(jt):
                                jsl = len(jq) % 3
                                koff = 1024 * (jt % 2) + 128 * (jt // 2)
                                kt = kh[:, sl, koff:koff + 128]
                                if len(jq) >= 3:
                                    b.w("te", jq[-3][1], jq[-3][0])
                                vst = b.i("te", lambda E, jsl=jsl, kt=kt, qs=qs:
                                          E.matmul(p_st[jsl][:, :], kt, qs,
                                                   start=True, stop=True), sem="te")
                                nonlocal act_gate
                                b.w("act", "te", vst)
                                if not act_gate:
                                    b.w("act", "te", vden_prev)
                                    act_gate = True
                                vex2 = b.i("act", lambda E, jsl=jsl, jt=jt:
                                           E.activation(pt[:, jt, :],
                                                        p_st[jsl][:, :],
                                                        AF.Exp, bias=SM_SHIFT,
                                                        scale=SM_SCALE), sem="act")
                                if jt >= 8 * m:
                                    kk = jt - 8 * m
                                    b.w("dve", "act", vex2)
                                    vmk = b.i("dve", lambda E, kk=kk, jt=jt:
                                              E.tensor_mul(pt[:, jt, :], pt[:, jt, :],
                                                           msk[:, kk * 512:(kk + 1) * 512]),
                                              sem="dve")
                                    jq.append((vmk, "dve"))
                                else:
                                    jq.append((vex2, "act"))

                            def emit_pv(jt, msl=m % 2, jmax=jmax):
                                koff = 1024 * (jt % 2) + 128 * (jt // 2)
                                vt = vh[:, sl, koff:koff + 128]
                                b.w("te", jq[base + jt][1], jq[base + jt][0])
                                b.i("te", lambda E, msl=msl, jt=jt, vt=vt, jmax=jmax:
                                    E.matmul(p_ov[msl][:, :], vt, pt[:, jt, :],
                                             start=(jt == 0), stop=(jt == jmax - 1)),
                                    sem=("te" if jt == jmax - 1 else None))

                            base = len(jq)
                            for jt in range(jmax):
                                emit_st(jt)
                                if jt >= 2:
                                    emit_pv(jt - 2)
                            emit_pv(jmax - 2)
                            emit_pv(jmax - 1)
                            msl = m % 2
                            b.w("te", "dve", vrr_last)
                            for jt in range(jmax):
                                vden_last = b.i(
                                    "te", lambda E, jt=jt, jmax=jmax:
                                    E.matmul(p_den[0:1, :],
                                             ones_cb[:, :], pt[:, jt, :],
                                             start=(jt == 0), stop=(jt == jmax - 1)),
                                    sem=("te" if jt == jmax - 1 else None))
                            vj = b.cnt["te"]
                            b.w("dve", "te", vj)
                            def _recipb(E, msl=msl):
                                with nc.allow_low_precision(reason="f32r recip row"):
                                    return E.reciprocal(rrow[0:1, msl * 512:(msl + 1) * 512],
                                                        p_den[0:1, :])
                            vrr_last = b.i("dve", _recipb, sem="dve")
                            b.w("te", "dve", vrr_last)
                            b.w("te", "dve", vrc_last)
                            vbc = b.i("te", lambda E, msl=msl:
                                      E.matmul(p_bc[:, :], ones_r[0:1, 0:128],
                                               rrow[0:1, msl * 512:(msl + 1) * 512],
                                               start=True, stop=True), sem="te")
                            b.w("dve", "te", vbc)
                            vrc_last = b.i("dve", lambda E, msl=msl:
                                           E.tensor_copy(rb_sb[:, msl * 512:(msl + 1) * 512],
                                                         p_bc[:, :]), sem="dve")
                            b.w("dve", "dve", vrc_last)
                            mden.append(b.i("dve", lambda E, msl=msl, h=h, m=m:
                                            E.scalar_tensor_tensor(
                                                sq[:, h * 1024 + 512 * m: h * 1024 + 512 * (m + 1)],
                                                p_ov[msl][:, :], F0,
                                                rb_sb[:, msl * 512:(msl + 1) * 512],
                                                ALU.bypass, ALU.mult), sem="dve"))
                        hb_te.append(b.cnt["te"])
                    vattn = b.cnt["dve"]

                with contextlib.ExitStack() as ps:
                    p_o = [ps.enter_context(nc.psum_tensor(f"pob{i}", [128, 512], F32))
                           for i in range(2)]
                    ob_te = []
                    o_ev = []
                    for oc in range(NCH):
                        sl = oc % 2
                        if oc >= 2:
                            b.w("gp", "te", ob_te[2 * (oc - 2) + 1])
                        else:
                            b.w("gp", "te", qb_te[2 * (14 + oc) + 1])
                        vld = b.i("gp", lambda E, oc=oc, sl=sl: E.dma_start(
                            out=qp[:, sl, :],
                            in_=dap(d_opt, oc * 128 * 2048, [[2048, 128], [1, 2048]])),
                            sem=f"s_qp{sl}", n=16)
                        b.w("te", f"s_qp{sl}", vld)
                        if oc == 0:
                            b.w("te", "dve", vattn)
                        for T in range(2):
                            gsl = (2 * oc + T) % 2
                            if len(o_ev) >= 2:
                                b.w("te", "dve", o_ev[-2])
                            vte = None
                            for c in range(NCH):
                                vte = b.i("te", lambda E, gsl=gsl, sl=sl, c=c, T=T:
                                          E.matmul(p_o[gsl][:, :],
                                                   qp[:, sl, c * 128:(c + 1) * 128],
                                                   sq[:, c * 1024 + T * 512: c * 1024 + (T + 1) * 512],
                                                   start=(c == 0), stop=(c == 15)),
                                          sem=("te" if c == 15 else None))
                            ob_te.append(vte)
                            b.w("dve", "te", vte)
                            o_ev.append(b.i("dve", lambda E, gsl=gsl, oc=oc, T=T:
                                            E.tensor_add(
                                                x_sb[:, oc * 1024 + T * 512: oc * 1024 + (T + 1) * 512],
                                                x_sb[:, oc * 1024 + T * 512: oc * 1024 + (T + 1) * 512],
                                                p_o[gsl][:, :]), sem="dve"))
                return o_ev

        pred = None
        if upto >= 1:
            pred = emit_attn_a(pred=pred)
            b.qbar()
        if upto >= 2:
            pred = emit_ffn(1, pred=pred)
            b.qbar()
        if upto >= 3:
            pred = emit_attn_b(pred=pred)
            b.qbar()
        if upto >= 4:
            pred = emit_ffn(2, pred=pred)
            b.qbar()

        if upto >= 4:
            b.w("gp", "st", b.cnt["st"])
        else:
            b.w("sy", "dve", b.cnt["dve"])
            b.i("sy", lambda E: E.dma_start(
                out=dap(d_y, 0, [[1024, 128], [131072, 16], [1, 1024]]),
                in_=x_sb.ap()), sem="st", n=16)
            b.w("sy", "st", b.cnt["st"])

        with nc.Block() as block:
            b.replay(block)

    return nc



def _tile4(w):
    return np.ascontiguousarray(
        w.reshape(16, 128, 16, 128).transpose(0, 3, 2, 1).reshape(16, 128, 2048))


def _build_masks(p):
    import ml_dtypes
    out = np.ones((8, 128, 512), dtype=ml_dtypes.bfloat16)
    jr = np.arange(128)[:, None]
    ic = np.arange(128)[None, :]
    for k in range(8):
        for qq in range(4):
            delta = k - 2 * qq - p
            blk = out[k, :, qq * 128:(qq + 1) * 128]
            if delta == 0:
                blk[:] = np.where(jr > ic, 0.0, 1.0).astype(ml_dtypes.bfloat16)
            elif delta > 0:
                blk[:] = ml_dtypes.bfloat16(0.0)
            else:
                blk[:] = ml_dtypes.bfloat16(1.0)
    return out


def kernel(**inputs):
    x = np.ascontiguousarray(np.asarray(inputs["L0_attn_a_hidden_states"], dtype=np.float32))
    wq = np.asarray(inputs["L0_attn_a_Wq"], dtype=np.float32)
    bq = np.asarray(inputs["L0_attn_a_bq"], dtype=np.float32)
    wk = np.asarray(inputs["L0_attn_a_Wk"], dtype=np.float32)
    bk = np.asarray(inputs["L0_attn_a_bk"], dtype=np.float32)
    wv = np.asarray(inputs["L0_attn_a_Wv"], dtype=np.float32)
    bv = np.asarray(inputs["L0_attn_a_bv"], dtype=np.float32)
    wo = np.asarray(inputs["L0_attn_a_Wo"], dtype=np.float32)
    qp = np.asarray(inputs["L0_attn_b_q_proj"], dtype=np.float32)
    kp = np.asarray(inputs["L0_attn_b_k_proj"], dtype=np.float32)
    vp = np.asarray(inputs["L0_attn_b_v_proj"], dtype=np.float32)
    op = np.asarray(inputs["L0_attn_b_o_proj"], dtype=np.float32)
    wg = np.asarray(inputs["L0_ffn_w_gating"], dtype=np.float32)
    wl = np.asarray(inputs["L0_ffn_w_linear"], dtype=np.float32)

    shared = {
        "wqt": _tile4(wq),
        "wkvt": np.ascontiguousarray(np.concatenate([wk, wv], axis=0).T),
        "bq": np.ascontiguousarray(bq.reshape(1, 2048)),
        "bkv": np.ascontiguousarray(np.concatenate([bk, bv]).reshape(1, 1024)),
        "wot": _tile4(wo),
        "qpt": _tile4(qp),
        "kpt": _tile4(kp),
        "vpt": np.ascontiguousarray(vp.T),
        "opt": _tile4(op),
        "wgdr": np.ascontiguousarray(
            wg.reshape(2, 8, 2, 128, 64, 128).transpose(0, 4, 3, 1, 2, 5)
            .reshape(2, 64, 128, 2048)),
        "onesin": np.ones((128, 512), dtype=np.float32),
        "wldr": np.ascontiguousarray(
            wl.reshape(32, 2, 128, 16, 128).transpose(3, 2, 0, 1, 4)
            .reshape(16, 128, 8192)),
    }
    masks = [_build_masks(0), _build_masks(1)]

    in_maps = []
    for c in range(8):
        s, p = divmod(c, 2)
        rows = np.concatenate(
            [x[s, 128 * (2 * l + p):128 * (2 * l + p) + 128, :] for l in range(8)], axis=0)
        m = dict(shared)
        m["xfm"] = np.ascontiguousarray(rows.T)
        m["masks"] = masks[p]
        in_maps.append(m)

    nc = build_program(_UPTO)
    res = run_bass_kernel_spmd(nc, in_maps, core_ids=list(range(8)))

    y = np.empty((4, 2048, 2048), dtype=np.float32)
    for c in range(8):
        s, p = divmod(c, 2)
        tok = np.asarray(res.results[c]["yfm"]).T
        for l in range(8):
            y[s, 128 * (2 * l + p):128 * (2 * l + p) + 128, :] = tok[128 * l:128 * (l + 1), :]
    return y
